# version 01 (final)
# baseline (speedup 1.0000x reference)
import numpy as np
import ml_dtypes
from contextlib import ExitStack

import concourse.bass as bass
import concourse.mybir as mybir
from concourse.bass_utils import run_bass_kernel_spmd

F32 = mybir.dt.float32
BF16 = mybir.dt.bfloat16
AF = mybir.ActivationFunctionType
ALU = mybir.AluOpType
AX = mybir.AxisListType

S, D = 2048, 1024
NT = S // 128
ND = D // 128
NH, DH = 16, 64
EPS = 1e-5
BF = ml_dtypes.bfloat16

SUBS_ALL = ["l0aa", "l0ffn", "l0ab", "l0ffn2", "l1aa", "l1ffn", "l1ab", "l1ffn2"]
DBG_STAGE = 0
DBG_NOMASK = False
DBG_L1F = None
DBG_L0F = None
DBG_MOE = None

ST_SUMS, ST_SQS, ST_NEGM, ST_M2, ST_VAR, ST_RSTD, ST_NEGMR = 0, 16, 32, 48, 64, 80, 96
ST_EPS = 112
ST_T0, ST_T1 = 114, 130
ST_SV_R, ST_SV_NMR = 146, 162
ST_DEC = 192
ST_ONE = 208


class Kern:
    def __init__(self, nc, stack):
        self.nc = nc
        self.st = stack
        e = stack.enter_context
        self.resid = e(nc.sbuf_tensor("resid", [128, NT * 1024], F32))
        self.fm1 = e(nc.sbuf_tensor("fm1", [128, ND * 2048], BF16))
        self.fm2 = e(nc.sbuf_tensor("fm2", [128, ND * 2048], BF16))
        self.qk = e(nc.sbuf_tensor("qk", [128, 8192], BF16))
        self.vt = e(nc.sbuf_tensor("vt", [128, 4160], BF16))
        self.expt = e(nc.sbuf_tensor("expt", [128, 5 * 512], BF16))
        self.wbuf = e(nc.sbuf_tensor("wbuf", [128, 12288], BF16))
        self.scr = e(nc.sbuf_tensor("scr", [128, 2048], F32))
        self.xtok = e(nc.sbuf_tensor("xtok", [128, 2048], BF16))
        self.stats = e(nc.sbuf_tensor("stats", [128, 224], F32))
        self.pvec = e(nc.sbuf_tensor("pvec", [128, 640], F32))
        self.ident = e(nc.sbuf_tensor("ident", [128, 128], BF16))
        self.identf = e(nc.sbuf_tensor("identf", [128, 128], F32))
        self.ones_rb = e(nc.sbuf_tensor("ones_rb", [1, 128], BF16))
        self.ones_rf = e(nc.sbuf_tensor("ones_rf", [1, 128], F32))
        self.brow = e(nc.sbuf_tensor("brow", [1, 1536], BF16))
        self.misc = e(nc.sbuf_tensor("misc", [128, 1536], F32))
        self.rb = self.misc[0:64, 0:512]
        self.rowf = self.misc[0:1, 512:1536]
        self.bar = e(nc.semaphore("bar")); self.bar_n = 0
        self.n_tag = 0
        self.lw = e(nc.semaphore("lw")); self.lw_n = 0
        self.ld = e(nc.semaphore("ld")); self.ld_n = 0
        lo = max(self.bar.num, self.lw.num, self.ld.num) + 1
        self.clear_rng = range(lo, bass.get_kernel_semaphore_range().stop)
        self.debug = False

    def tag(self, base):
        self.n_tag += 1
        return f"{base}_{self.n_tag}"

    def barrier(self):
        self.bar_n += 2
        t1, t2 = 5 * (self.bar_n - 1), 5 * self.bar_n
        with self.nc.Block(f"bar{self.bar_n}") as b:
            for name in ("tensor", "vector", "scalar", "gpsimd", "sync"):
                def mk(eng, _t1=t1, _t2=t2, _g=(name == "gpsimd")):
                    eng.sem_inc(self.bar, 1)
                    eng.wait_ge(self.bar, _t1)
                    if _g:
                        eng.sem_clear(self.clear_rng)
                    eng.sem_inc(self.bar, 1)
                    eng.wait_ge(self.bar, _t2)
                getattr(b, name)(mk)


def qcol(blk, i, w=4):
    return blk[:, w * i:w * i + w]



def emit_init(K, consts):
    nc = K.nc
    K.ld_n += 4 * 16
    ld_tgt = K.ld_n
    with nc.Block("init") as b:
        @b.sync
        def _(sy):
            sy.dma_start(
                out=K.resid[:, :].rearrange("p (t d) -> p t d", t=NT),
                in_=consts["x"].rearrange("(t p) d -> p t d", p=128),
            ).then_inc(K.ld, 16)
            sy.dma_start(out=K.ident[:, :], in_=consts["c_ident"]).then_inc(K.ld, 16)
            sy.dma_start(out=K.identf[:, :], in_=consts["c_identf"]).then_inc(K.ld, 16)
            sy.dma_start(out=K.pvec[:, 0:consts["c_pvec"].shape[1]],
                         in_=consts["c_pvec"]).then_inc(K.ld, 16)
            sy.wait_ge(K.ld, ld_tgt)

        @b.gpsimd
        def _(g):
            g.memset(K.ones_rb[:, :], 1.0)
            g.memset(K.ones_rf[:, :], 1.0)
            g.memset(K.stats[:, ST_EPS:ST_EPS + 1], EPS)
            g.memset(K.stats[:, ST_ONE:ST_ONE + 1], 1.0)


def emit_store(K, out_ap):
    nc = K.nc
    K.ld_n += 16
    tgt = K.ld_n
    with nc.Block("store") as b:
        @b.sync
        def _(sy):
            sy.dma_start(
                out=out_ap.rearrange("(t p) d -> p t d", p=128),
                in_=K.resid[:, :].rearrange("p (t d) -> p t d", t=NT),
            ).then_inc(K.ld, 16)
            sy.wait_ge(K.ld, tgt)



def emit_ln(K, tag, dst=None, gb=None, save=False, inner_of_saved=False, no_out=False):
    tag = K.tag(tag)
    nc = K.nc
    dst = dst if dst is not None else K.fm1
    src = K.resid
    with ExitStack() as ps:
        e = ps.enter_context
        p_tr = e(nc.psum_tensor(f"pln_{tag}", [128, 2 * 1024], BF16))
        stv = e(nc.semaphore(f"stv_{tag}"))
        sts = e(nc.semaphore(f"sts_{tag}"))
        st2 = e(nc.semaphore(f"st2_{tag}"))
        nrm = e(nc.semaphore(f"nrm_{tag}"))
        tp = e(nc.semaphore(f"tp_{tag}"))
        ev = e(nc.semaphore(f"ev_{tag}"))

        st = K.stats
        sums = st[:, ST_SUMS:ST_SUMS + 16]
        sqs = st[:, ST_SQS:ST_SQS + 16]
        negmean = st[:, ST_NEGM:ST_NEGM + 16]
        m2 = st[:, ST_M2:ST_M2 + 16]
        var = st[:, ST_VAR:ST_VAR + 16]
        rstd = st[:, ST_RSTD:ST_RSTD + 16]
        negmr = st[:, ST_NEGMR:ST_NEGMR + 16]
        eps = st[:, ST_EPS:ST_EPS + 1]
        t0 = st[:, ST_T0:ST_T0 + 16]
        sv_r = st[:, ST_SV_R:ST_SV_R + 16]
        sv_nmr = st[:, ST_SV_NMR:ST_SV_NMR + 16]

        with nc.Block(f"ln_{tag}") as b:
            @b.vector
            def _(v):
                if inner_of_saved:
                    for i in range(4):
                        v.tensor_tensor(qcol(t0, i), qcol(sv_r, i), qcol(sv_r, i), ALU.mult)
                    for i in range(4):
                        ins = v.tensor_scalar(qcol(var, i), qcol(t0, i), -EPS, 1.0,
                                              ALU.mult, ALU.add)
                    ins.then_inc(stv, 1)
                    v.wait_ge(sts, 1)
                    for i in range(4):
                        v.tensor_tensor(qcol(t0, i), qcol(sv_r, i), qcol(rstd, i), ALU.mult)
                    for i in range(4):
                        v.tensor_tensor(qcol(negmr, i), qcol(sv_nmr, i), qcol(rstd, i), ALU.mult)
                    for i in range(4):
                        ins = v.tensor_copy(qcol(rstd, i), qcol(t0, i))
                    ins.then_inc(st2, 1)
                else:
                    for t in range(NT):
                        v.reduce_sum(sums[:, t:t + 1], src[:, t * 1024:(t + 1) * 1024], axis=AX.X)
                        v.scalar_tensor_tensor(K.scr[:, (t % 2) * 1024:(t % 2) * 1024 + 1024],
                                               src[:, t * 1024:(t + 1) * 1024], 1.0,
                                               src[:, t * 1024:(t + 1) * 1024],
                                               ALU.mult, ALU.mult,
                                               accum_out=sqs[:, t:t + 1])
                    for i in range(4):
                        v.tensor_scalar(qcol(negmean, i), qcol(sums, i), -1.0 / 1024, None, ALU.mult)
                    for i in range(4):
                        v.tensor_tensor(qcol(m2, i), qcol(negmean, i), qcol(negmean, i), ALU.mult)
                    for i in range(4):
                        ins = v.scalar_tensor_tensor(qcol(var, i), qcol(sqs, i), 1.0 / 1024,
                                                     qcol(m2, i), ALU.mult, ALU.subtract)
                    ins.then_inc(stv, 1)
                    v.wait_ge(sts, 1)
                    for i in range(4):
                        ins = v.tensor_tensor(qcol(negmr, i), qcol(negmean, i), qcol(rstd, i), ALU.mult)
                    ins.then_inc(st2, 1)
                if no_out:
                    return
                for t in range(NT):
                    v.wait_ge(tp, t + 1)
                    pb = p_tr[:, (t % 2) * 1024:(t % 2) * 1024 + 1024]
                    if gb is None:
                        dv = dst[:, :].rearrange("p (c x) -> p c x", c=ND)[:, :, t * 128:(t + 1) * 128]
                        ins = v.tensor_copy(dv, pb.rearrange("p (c x) -> p c x", c=ND))
                    else:
                        goff, boff = gb
                        for c in range(ND):
                            ins = v.tensor_scalar(
                                dst[:, c * 2048 + t * 128: c * 2048 + (t + 1) * 128],
                                pb[:, c * 128:(c + 1) * 128],
                                K.pvec[:, goff + c:goff + c + 1],
                                K.pvec[:, boff + c:boff + c + 1],
                                ALU.mult, ALU.add)
                    ins.then_inc(ev, 1)

            @b.scalar
            def _(s):
                s.wait_ge(stv, 1)
                for i in range(4):
                    s.activation(qcol(rstd, i), qcol(var, i), AF.Ln, bias=eps)
                for i in range(4):
                    ins = s.activation(qcol(rstd, i), qcol(rstd, i), AF.Exp, scale=-0.5)
                ins.then_inc(sts, 1)
                s.wait_ge(st2, 1)
                if save:
                    s.copy(sv_r[:, :], rstd[:, :])
                    s.copy(sv_nmr[:, :], negmr[:, :])
                if no_out:
                    return
                for t in range(NT):
                    if t >= 2:
                        s.wait_ge(tp, t - 1)
                    s.activation(K.xtok[:, (t % 2) * 1024:(t % 2) * 1024 + 1024],
                                 src[:, t * 1024:(t + 1) * 1024], AF.Identity,
                                 bias=negmr[:, t:t + 1], scale=rstd[:, t:t + 1]).then_inc(nrm, 1)

            @b.tensor
            def _(tn):
                if no_out:
                    return
                for t in range(NT):
                    tn.wait_ge(nrm, t + 1)
                    if t >= 2:
                        tn.wait_ge(ev, t - 1)
                    pb = p_tr[:, (t % 2) * 1024:(t % 2) * 1024 + 1024]
                    for c in range(ND):
                        ins = tn.matmul(pb[:, c * 128:(c + 1) * 128],
                                        K.xtok[:, (t % 2) * 1024 + c * 128:(t % 2) * 1024 + (c + 1) * 128],
                                        K.ident[:, :], is_transpose=True,
                                        start=(c == 0), stop=(c == ND - 1))
                    ins.then_inc(tp, 1)



def emit_l0_attn_a(K, W, BIAS):
    nc = K.nc
    K.lw_n += 2 * 16
    lw_tgt = K.lw_n
    with ExitStack() as ps:
        e = ps.enter_context
        p_h = e(nc.psum_tensor("paa", [128, 2 * 1024], F32))
        mm = e(nc.semaphore("aa_mm"))
        sc = e(nc.semaphore("aa_sc"))
        ga = e(nc.semaphore("aa_ga"))
        onesb = K.fm2[:, 0:1024]
        with nc.Block("l0aa") as b:
            @b.gpsimd
            def _(g):
                g.memset(onesb, 1.0)
                g.dma_start(out=K.wbuf[:, 0:8192].rearrange("p (c m) -> p c m", c=ND),
                            in_=W.rearrange("(c p) m -> p c m", p=128)).then_inc(K.lw, 16)
                g.dma_start(out=K.brow[0:1, 0:1024], in_=BIAS).then_inc(K.lw, 16)
                for t in range(NT):
                    g.wait_ge(sc, t + 1)
                    g.tensor_add(K.resid[:, t * 1024:(t + 1) * 1024],
                                 K.resid[:, t * 1024:(t + 1) * 1024],
                                 K.scr[:, (t % 2) * 1024:(t % 2) * 1024 + 1024]).then_inc(ga, 1)

            @b.tensor
            def _(tn):
                tn.wait_ge(K.lw, lw_tgt)
                for t in range(NT):
                    if t >= 2:
                        tn.wait_ge(sc, t - 1)
                    pb = p_h[:, (t % 2) * 1024:(t % 2) * 1024 + 1024]
                    for half in range(2):
                        for kc in range(ND):
                            tn.matmul(pb[:, half * 512:half * 512 + 512],
                                      K.fm1[:, kc * 2048 + t * 128: kc * 2048 + t * 128 + 128],
                                      K.wbuf[:, kc * 1024 + half * 512: kc * 1024 + half * 512 + 512],
                                      start=(kc == 0), stop=False)
                        ins = tn.matmul(pb[:, half * 512:half * 512 + 512],
                                        K.ones_rb[0:1, 0:128],
                                        K.brow[0:1, half * 512:half * 512 + 512],
                                        start=False, stop=True)
                    ins.then_inc(mm, 1)

            @b.vector
            def _(v):
                for t in range(NT):
                    v.wait_ge(mm, t + 1)
                    if t >= 2:
                        v.wait_ge(ga, t - 1)
                    pb = p_h[:, (t % 2) * 1024:(t % 2) * 1024 + 1024]
                    v.tensor_tensor_scan(K.scr[:, (t % 2) * 1024:(t % 2) * 1024 + 1024],
                                         onesb, pb, 0.0, ALU.mult, ALU.add).then_inc(sc, 1)



def emit_qkv_quarter(K, tag, qtr, wq, wk, wv, bq_off=None, bk_off=None, bv_row=None,
                     src=None, n_tok=NT):
    tag = K.tag(tag)
    nc = K.nc
    src = src if src is not None else K.fm1
    K.lw_n += (3 + (1 if bv_row is not None else 0)) * 16
    lw_tgt = K.lw_n
    with ExitStack() as ps:
        e = ps.enter_context
        p_q = e(nc.psum_tensor(f"pq_{tag}{qtr}", [128, 4 * 512], F32))
        mmq = e(nc.semaphore(f"mmq_{tag}{qtr}"))
        evq = e(nc.semaphore(f"evq_{tag}{qtr}"))

        with nc.Block(f"qkv_{tag}{qtr}") as b:
            @b.gpsimd
            def _(g):
                for wi, w in enumerate((wq, wk, wv)):
                    g.dma_start(
                        out=K.wbuf[:, wi * 2048:(wi + 1) * 2048].rearrange("p (c m) -> p c m", c=ND),
                        in_=w.rearrange("(c p) m -> p c m", p=128)).then_inc(K.lw, 16)
                if bv_row is not None:
                    g.dma_start(out=K.brow[0:1, 1024:1280], in_=bv_row).then_inc(K.lw, 16)
                g.memset(K.vt[:, 0:n_tok * 4 * 65]
                         .rearrange("p (n o) -> p n o", o=65)[:, :, 64:65], 1.0)

            @b.tensor
            def _(tn):
                tn.wait_ge(K.lw, lw_tgt)
                i = 0
                for wi in range(2):
                    for P in range(2):
                        for tc in range(4):
                            if i >= 4:
                                tn.wait_ge(evq, i - 3)
                            pb = p_q[:, (i % 4) * 512:(i % 4) * 512 + 512]
                            for kc in range(ND):
                                ins = tn.matmul(
                                    pb,
                                    K.wbuf[:, wi * 2048 + kc * 256 + P * 128: wi * 2048 + kc * 256 + (P + 1) * 128],
                                    src[:, kc * 2048 + tc * 512: kc * 2048 + (tc + 1) * 512],
                                    start=(kc == 0), stop=(kc == ND - 1))
                            ins.then_inc(mmq, 1)
                            i += 1
                for t in range(n_tok):
                    if i >= 4:
                        tn.wait_ge(evq, i - 3)
                    pb = p_q[:, (i % 4) * 512:(i % 4) * 512 + 256]
                    for kc in range(ND):
                        ins = tn.matmul(pb,
                                        src[:, kc * 2048 + t * 128: kc * 2048 + (t + 1) * 128],
                                        K.wbuf[:, 2 * 2048 + kc * 256: 2 * 2048 + (kc + 1) * 256],
                                        start=(kc == 0), stop=(kc == ND - 1 and bv_row is None))
                    if bv_row is not None:
                        ins = tn.matmul(pb, K.ones_rb[0:1, 0:128],
                                        K.brow[0:1, 1024:1280], start=False, stop=True)
                    ins.then_inc(mmq, 1)
                    i += 1

            @b.vector
            def _(v):
                i = 0
                for wi in range(2):
                    base = wi * 4096
                    for P in range(2):
                        for tc in range(4):
                            v.wait_ge(mmq, i + 1)
                            pb = p_q[:, (i % 4) * 512:(i % 4) * 512 + 512]
                            boff = (bq_off, bk_off)[wi]
                            dst = K.qk[:, base + P * 2048 + tc * 512: base + P * 2048 + (tc + 1) * 512]
                            if boff is None:
                                v.tensor_copy(dst, pb).then_inc(evq, 1)
                            else:
                                v.tensor_single_scalar(dst, pb, K.pvec[:, boff + P:boff + P + 1],
                                                       ALU.add).then_inc(evq, 1)
                            i += 1
                for t in range(n_tok):
                    v.wait_ge(mmq, i + 1)
                    pb = p_q[:, (i % 4) * 512:(i % 4) * 512 + 256]
                    dstv = K.vt[:, 0:n_tok * 4 * 65].rearrange("p (n o) -> p n o", o=65)[:, t * 4:(t + 1) * 4, 0:64]
                    v.tensor_copy(dstv, pb.rearrange("p (h x) -> p h x", h=4)).then_inc(evq, 1)
                    i += 1


def emit_attn_core_quarter(K, tag, qtr, scale, n_kc=NT, causal=False):
    tag = K.tag(tag)
    nc = K.nc
    with ExitStack() as ps:
        e = ps.enter_context
        p_sc = e(nc.psum_tensor(f"psc_{tag}{qtr}", [128, 2 * 512], F32))
        p_av = e(nc.psum_tensor(f"pav_{tag}{qtr}", [128, 2 * 512], F32))
        p_rb = e(nc.psum_tensor(f"prb_{tag}{qtr}", [128, 2 * 512], F32))
        scm = e(nc.semaphore(f"scm_{tag}{qtr}"))
        exd = e(nc.semaphore(f"exd_{tag}{qtr}"))
        avm = e(nc.semaphore(f"avm_{tag}{qtr}"))
        rcv = e(nc.semaphore(f"rcv_{tag}{qtr}"))
        rbm = e(nc.semaphore(f"rbm_{tag}{qtr}"))
        rbe = e(nc.semaphore(f"rbe_{tag}{qtr}"))
        ctd = e(nc.semaphore(f"ctd_{tag}{qtr}"))

        iters = [(hl, qc) for hl in range(4) for qc in range(4)]

        def kc_list(qc):
            if causal:
                return list(range(min(n_kc, (qc + 1) * 4)))
            return list(range(n_kc))

        with nc.Block(f"attn_{tag}{qtr}") as b:
            @b.tensor
            def _(tn):
                nsc = 0
                for it, (hl, qc) in enumerate(iters):
                    P = hl // 2
                    r0 = 64 * (hl % 2)
                    kcs = kc_list(qc)
                    if it >= 1:
                        tn.wait_ge(rcv, it)
                        if it >= 3:
                            tn.wait_ge(rbe, it - 2)
                        tn.matmul(p_rb[0:64, ((it - 1) % 2) * 512:((it - 1) % 2) * 512 + 512],
                                  K.ones_rf[0:1, 0:64],
                                  K.rowf[0:1, ((it - 1) % 2) * 512:((it - 1) % 2) * 512 + 512],
                                  start=True, stop=True).then_inc(rbm, 1)
                    if it >= 2:
                        tn.wait_ge(ctd, it - 1)
                    for j, kc in enumerate(kcs):
                        tn.matmul(p_sc[:, (nsc % 2) * 512:(nsc % 2) * 512 + 512],
                                  K.qk[r0:r0 + 64, 4096 + P * 2048 + kc * 128: 4096 + P * 2048 + (kc + 1) * 128],
                                  K.qk[r0:r0 + 64, P * 2048 + qc * 512: P * 2048 + (qc + 1) * 512],
                                  start=True, stop=True).then_inc(scm, 1)
                        tn.wait_ge(exd, nsc + 1)
                        ins = tn.matmul(p_av[0:65, (it % 2) * 512:(it % 2) * 512 + 512],
                                        K.vt[:, (kc * 4 + hl) * 65:(kc * 4 + hl) * 65 + 65],
                                        K.expt[:, (nsc % 3) * 512:(nsc % 3) * 512 + 512],
                                        start=(j == 0), stop=(j == len(kcs) - 1))
                        nsc += 1
                    ins.then_inc(avm, 1)
                it = len(iters)
                tn.wait_ge(rcv, it)
                tn.wait_ge(rbe, it - 2)
                tn.matmul(p_rb[0:64, ((it - 1) % 2) * 512:((it - 1) % 2) * 512 + 512],
                          K.ones_rf[0:1, 0:64],
                          K.rowf[0:1, ((it - 1) % 2) * 512:((it - 1) % 2) * 512 + 512],
                          start=True, stop=True).then_inc(rbm, 1)

            @b.scalar
            def _(s):
                nsc = 0
                for it, (hl, qc) in enumerate(iters):
                    for j, kc in enumerate(kc_list(qc)):
                        s.wait_ge(scm, nsc + 1)
                        s.activation(K.expt[:, (nsc % 3) * 512:(nsc % 3) * 512 + 512],
                                     p_sc[:, (nsc % 2) * 512:(nsc % 2) * 512 + 512],
                                     AF.Exp, scale=scale).then_inc(exd, 1)
                        nsc += 1
                    s.wait_ge(rbm, it + 1)
                    if it >= 1:
                        s.wait_ge(ctd, it)
                    s.copy(K.rb[0:64, :], p_rb[0:64, (it % 2) * 512:(it % 2) * 512 + 512]).then_inc(rbe, 1)

            @b.vector
            def _(v):
                for it, (hl, qc) in enumerate(iters):
                    P = hl // 2
                    r0 = 64 * (hl % 2)
                    gp = qtr * 2 + P
                    v.wait_ge(avm, it + 1)
                    v.reciprocal(K.rowf[0:1, (it % 2) * 512:(it % 2) * 512 + 512],
                                 p_av[64:65, (it % 2) * 512:(it % 2) * 512 + 512]).then_inc(rcv, 1)
                    v.wait_ge(rbe, it + 1)
                    v.tensor_mul(K.fm2[r0:r0 + 64, gp * 2048 + qc * 512: gp * 2048 + (qc + 1) * 512],
                                 p_av[0:64, (it % 2) * 512:(it % 2) * 512 + 512],
                                 K.rb[0:64, :]).then_inc(ctd, 1)



def emit_l0ffn_rprep(K, perev, wr):
    nc = K.nc
    K.lw_n += 5 * 16
    lw_base = K.lw_n - 5 * 16
    with ExitStack() as ps:
        e = ps.enter_context
        p_r = e(nc.psum_tensor("prr", [128, 4 * 512], F32))
        mmr = e(nc.semaphore("rr_mm"))
        evr = e(nc.semaphore("rr_ev"))
        str_ = e(nc.semaphore("rr_sto"))
        with nc.Block("rprep") as b:
            @b.gpsimd
            def _(g):
                g.dma_start(out=K.wbuf[:, 0:8192].rearrange("p (c m) -> p c m", c=ND),
                            in_=wr.rearrange("(c p) m -> p c m", p=128)).then_inc(K.lw, 16)
                for mg in range(4):
                    if mg >= 1:
                        g.wait_ge(mmr, mg * 8)
                    g.dma_start(out=K.wbuf[:, 8192:12288].rearrange("p (c m) -> p c m", c=ND),
                                in_=perev[:, mg * 512:(mg + 1) * 512]
                                .rearrange("(c p) m -> p c m", p=128)).then_inc(K.lw, 16)

            @b.tensor
            def _(tn):
                for mg in range(4):
                    tn.wait_ge(K.lw, lw_base + 16 * (mg + 2))
                    for P in range(8):
                        i = mg * 8 + P
                        if i >= 4:
                            tn.wait_ge(evr, i - 3)
                        pb = p_r[:, (i % 4) * 512:(i % 4) * 512 + 512]
                        for kc in range(ND):
                            ins = tn.matmul(pb,
                                            K.wbuf[:, kc * 1024 + P * 128: kc * 1024 + (P + 1) * 128],
                                            K.wbuf[:, 8192 + kc * 512: 8192 + (kc + 1) * 512],
                                            start=(kc == 0), stop=(kc == ND - 1))
                        ins.then_inc(mmr, 1)

            @b.vector
            def _(v):
                for i in range(32):
                    v.wait_ge(mmr, i + 1)
                    if i >= 2:
                        v.wait_ge(str_, 16 * (i - 1))
                    v.tensor_copy(K.expt[:, (i % 2) * 512:(i % 2) * 512 + 512],
                                  p_r[:, (i % 4) * 512:(i % 4) * 512 + 512]).then_inc(evr, 1)

            @b.sync
            def _(sy):
                for i in range(32):
                    mg, P = i // 8, i % 8
                    sy.wait_ge(evr, i + 1)
                    sy.dma_start(out=K.dr_rr[P][:, mg * 512:(mg + 1) * 512],
                                 in_=K.expt[:, (i % 2) * 512:(i % 2) * 512 + 512]).then_inc(str_, 16)
                sy.wait_ge(str_, 16 * 32)


def emit_l0ffn_attn_quarter(K, tag, qtr, dvoff):
    tag = K.tag(tag)
    nc = K.nc
    P_Z = 2560
    K.lw_n += 2 * 16
    lw_base = K.lw_n - 32
    with ExitStack() as ps:
        e = ps.enter_context
        p_sc = e(nc.psum_tensor(f"psc_{tag}", [128, 2 * 512], F32))
        p_av = e(nc.psum_tensor(f"pav_{tag}", [128, 2 * 512], F32))
        p_rb = e(nc.psum_tensor(f"prb_{tag}", [128, 2 * 512], F32))
        p_z = e(nc.psum_tensor(f"pz_{tag}", [128, 2 * 512], F32))
        scm = e(nc.semaphore(f"scm_{tag}"))
        expc = e(nc.semaphore(f"expc_{tag}"))
        exd = e(nc.semaphore(f"exd_{tag}"))
        avm = e(nc.semaphore(f"avm_{tag}"))
        rcv = e(nc.semaphore(f"rcv_{tag}"))
        rbm = e(nc.semaphore(f"rbm_{tag}"))
        rbe = e(nc.semaphore(f"rbe_{tag}"))
        ctd = e(nc.semaphore(f"ctd_{tag}"))
        zmm = e(nc.semaphore(f"zmm_{tag}"))
        zev = e(nc.semaphore(f"zev_{tag}"))
        zst = e(nc.semaphore(f"zst_{tag}"))
        bld = e(nc.semaphore(f"bld_{tag}"))
        btr = e(nc.semaphore(f"btr_{tag}"))
        qvd = e(nc.semaphore(f"qvd_{tag}"))

        iters = [(hl, qc) for hl in range(4) for qc in range(4)]
        NI = len(iters)

        def m_chunks(qc):
            return list(range(4 - (qc + 1), 4))

        def kcs_of(qc):
            return list(range(4 * qc + 4))

        nz_of = [4 * (qc + 1) for (hl, qc) in iters]
        cz = [sum(nz_of[:i]) for i in range(NI + 1)]
        cbd = cz
        csc = [sum(len(kcs_of(qc)) for (_, qc) in iters[:i]) for i in range(NI + 1)]
        zsl = K.wbuf[:, 4096:5120]

        def rb_mm(tn, j):
            tn.wait_ge(rcv, j + 1)
            if j >= 2:
                tn.wait_ge(rbe, j - 1)
            tn.matmul(p_rb[0:64, (j % 2) * 512:(j % 2) * 512 + 512],
                      K.ones_rf[0:1, 0:64],
                      K.rowf[0:1, (j % 2) * 512:(j % 2) * 512 + 512],
                      start=True, stop=True).then_inc(rbm, 1)

        with nc.Block(f"xl_{tag}") as b:
            @b.gpsimd
            def _(g):
                for P in range(2):
                    g.dma_start(out=K.wbuf[:, P * 2048:(P + 1) * 2048],
                                in_=K.dr_rr[qtr * 2 + P]).then_inc(K.lw, 16)
                for step in range(NI + 1):
                    if step >= 1:
                        hl, qc = iters[step - 1]
                        nsc0 = csc[step - 1]
                        for j, kc in enumerate(kcs_of(qc)):
                            if kc >= 4 * qc:
                                g.wait_ge(expc, nsc0 + j + 1)
                                g.affine_select(K.expt[:, ((nsc0 + j) % 4) * 512:((nsc0 + j) % 4) * 512 + 512],
                                                K.expt[:, ((nsc0 + j) % 4) * 512:((nsc0 + j) % 4) * 512 + 512],
                                                [[1, 512]], ALU.is_ge, 0.0,
                                                base=qc * 512 - kc * 128,
                                                channel_multiplier=-1).then_inc(exd, 1)
                    if step < NI:
                        hl, qc = iters[step]
                        par = step % 2
                        moff = (4 - (qc + 1)) * 512
                        g.wait_ge(zst, 16 * cz[step + 1])
                        for kcg in range(qc + 1):
                            for qs in range(4):
                                idx = cbd[step] + kcg * 4 + qs
                                if idx >= 4:
                                    g.wait_ge(btr, idx - 3)
                                qt0 = qc * 512 + qs * 128
                                base_off = par * 512 * P_Z + (qs * 128) * P_Z + (2047 - qt0) + kcg * 512 - moff
                                g.dma_start(out=K.scr[:, (idx % 4) * 512:(idx % 4) * 512 + 512],
                                            in_=bass.AP(K.dr_z, base_off, [[P_Z - 1, 128], [1, 512]])
                                            ).then_inc(bld, 16)

            @b.tensor
            def _(tn):
                nsc = 0
                for step in range(NI + 1):
                    if step >= 2:
                        rb_mm(tn, step - 2)
                    if step < NI:
                        hl, qc = iters[step]
                        P, r0 = hl // 2, 64 * (hl % 2)
                        tn.wait_ge(K.lw, lw_base + 32)
                        tn.wait_ge(qvd, 4 * (step + 1))
                        nzl = 0
                        for qs in range(4):
                            for mg in m_chunks(qc):
                                nglob = cz[step] + nzl
                                if nglob >= 2:
                                    tn.wait_ge(zev, nglob - 1)
                                tn.matmul(p_z[:, (nglob % 2) * 512:(nglob % 2) * 512 + 512],
                                          K.xtok[r0:r0 + 64, (step % 2) * 1024 + qs * 128:(step % 2) * 1024 + (qs + 1) * 128],
                                          K.wbuf[r0:r0 + 64, P * 2048 + mg * 512: P * 2048 + (mg + 1) * 512],
                                          start=True, stop=True).then_inc(zmm, 1)
                                nzl += 1
                    if step >= 1:
                        hl, qc = iters[step - 1]
                        ita = step - 1
                        P, r0 = hl // 2, 64 * (hl % 2)
                        kcs = kcs_of(qc)
                        if ita >= 2:
                            tn.wait_ge(ctd, ita - 1)
                        for kcg in range(qc + 1):
                            gk = [kc for kc in kcs if kc // 4 == kcg]
                            for kc in gk:
                                if nsc >= 2:
                                    tn.wait_ge(expc, nsc - 1)
                                tn.matmul(p_sc[:, (nsc % 2) * 512:(nsc % 2) * 512 + 512],
                                          K.qk[r0:r0 + 64, 4096 + P * 2048 + kc * 128: 4096 + P * 2048 + (kc + 1) * 128],
                                          K.qk[r0:r0 + 64, P * 2048 + qc * 512: P * 2048 + (qc + 1) * 512],
                                          start=True, stop=False)
                                for qs in range(4):
                                    idx = cbd[ita] + kcg * 4 + qs
                                    tn.wait_ge(bld, 16 * (idx + 1))
                                    ins = tn.matmul(p_sc[:, (nsc % 2) * 512 + qs * 128:(nsc % 2) * 512 + (qs + 1) * 128],
                                                    K.scr[:, (idx % 4) * 512 + (kc % 4) * 128:(idx % 4) * 512 + (kc % 4 + 1) * 128],
                                                    K.identf[:, :], is_transpose=True,
                                                    start=False, stop=(qs == 3))
                                    if qs == 3:
                                        ins.then_inc(scm, 1)
                                nsc += 1
                            tn.sem_inc(btr, 4)
                            for kc in gk:
                                nl = csc[ita] + kc
                                if kc >= 4 * qc:
                                    tn.wait_ge(exd, 4 * ita + (kc - 4 * qc) + 1)
                                else:
                                    tn.wait_ge(expc, nl + 1)
                                ins = tn.matmul(p_av[0:65, (ita % 2) * 512:(ita % 2) * 512 + 512],
                                                K.vt[:, (kc * 4 + hl) * 65:(kc * 4 + hl) * 65 + 65],
                                                K.expt[:, (nl % 4) * 512:(nl % 4) * 512 + 512],
                                                start=(kc == 0), stop=(kc == len(kcs) - 1))
                        ins.then_inc(avm, 1)
                rb_mm(tn, NI - 1)

            @b.scalar
            def _(s):
                nsc = 0
                for step in range(NI + 1):
                    if step < NI:
                        hl, qc = iters[step]
                        P, r0 = hl // 2, 64 * (hl % 2)
                        if step >= 2:
                            s.wait_ge(zmm, cz[step - 1])
                        for qs in range(4):
                            s.activation(K.xtok[r0:r0 + 64, (step % 2) * 1024 + qs * 128:(step % 2) * 1024 + (qs + 1) * 128],
                                         K.qk[r0:r0 + 64, P * 2048 + qc * 512 + qs * 128: P * 2048 + qc * 512 + (qs + 1) * 128],
                                         AF.Identity,
                                         bias=K.pvec[r0:r0 + 64, dvoff + 2 * qtr + P:dvoff + 2 * qtr + P + 1]
                                         ).then_inc(qvd, 1)
                        for nzl in range(cz[step + 1] - cz[step]):
                            nglob = cz[step] + nzl
                            s.wait_ge(zmm, nglob + 1)
                            if nglob >= 2:
                                s.wait_ge(zst, 16 * (nglob - 1))
                            s.copy(zsl[:, (nglob % 2) * 512:(nglob % 2) * 512 + 512],
                                   p_z[:, (nglob % 2) * 512:(nglob % 2) * 512 + 512]
                                   ).then_inc(zev, 1)
                    if step >= 1:
                        hl, qc = iters[step - 1]
                        ita = step - 1
                        for j, kc in enumerate(kcs_of(qc)):
                            s.wait_ge(scm, nsc + 1)
                            s.activation(K.expt[:, (nsc % 4) * 512:(nsc % 4) * 512 + 512],
                                         p_sc[:, (nsc % 2) * 512:(nsc % 2) * 512 + 512],
                                         AF.Exp, scale=0.125).then_inc(expc, 1)
                            nsc += 1
                        s.wait_ge(rbm, ita + 1)
                        if ita >= 1:
                            s.wait_ge(ctd, ita)
                        s.copy(K.rb[0:64, :], p_rb[0:64, (ita % 2) * 512:(ita % 2) * 512 + 512]).then_inc(rbe, 1)

            @b.sync
            def _(sy):
                for it, (hl, qc) in enumerate(iters):
                    par = it % 2
                    moff = (4 - (qc + 1)) * 512
                    nzl = 0
                    for qs in range(4):
                        for mg in m_chunks(qc):
                            nglob = cz[it] + nzl
                            sy.wait_ge(zev, nglob + 1)
                            off = par * 512 * P_Z + (qs * 128) * P_Z + mg * 512 - moff
                            sy.dma_start(out=bass.AP(K.dr_z, off, [[P_Z, 128], [1, 512]]),
                                         in_=zsl[:, (nglob % 2) * 512:(nglob % 2) * 512 + 512]
                                         ).then_inc(zst, 16)
                            nzl += 1

            @b.vector
            def _(v):
                for ita, (hl, qc) in enumerate(iters):
                    P = hl // 2
                    r0 = 64 * (hl % 2)
                    gp = qtr * 2 + P
                    v.wait_ge(avm, ita + 1)
                    v.reciprocal(K.rowf[0:1, (ita % 2) * 512:(ita % 2) * 512 + 512],
                                 p_av[64:65, (ita % 2) * 512:(ita % 2) * 512 + 512]).then_inc(rcv, 1)
                    v.wait_ge(rbe, ita + 1)
                    v.tensor_mul(K.fm2[r0:r0 + 64, gp * 2048 + qc * 512: gp * 2048 + (qc + 1) * 512],
                                 p_av[0:64, (ita % 2) * 512:(ita % 2) * 512 + 512],
                                 K.rb[0:64, :]).then_inc(ctd, 1)


def emit_l0ffn_y(K, tag, wo):
    tag = K.tag(tag)
    nc = K.nc
    st = K.stats
    K.lw_n += 16
    lw_tgt = K.lw_n
    with ExitStack() as ps:
        e = ps.enter_context
        p_o = e(nc.psum_tensor(f"po_{tag}", [128, 4 * 512], F32))
        mmo = e(nc.semaphore(f"mmo_{tag}"))
        zs = e(nc.semaphore(f"zs_{tag}"))
        sm = e(nc.semaphore(f"sm_{tag}"))
        with nc.Block(f"yproj_{tag}") as b:
            @b.gpsimd
            def _(g):
                g.dma_start(out=K.wbuf[:, 0:8192].rearrange("p (c m) -> p c m", c=ND),
                            in_=wo.rearrange("(c p) m -> p c m", p=128)).then_inc(K.lw, 16)

            @b.tensor
            def _(tn):
                tn.wait_ge(K.lw, lw_tgt)
                for i in range(NT * 2):
                    t, dg = i // 2, i % 2
                    if i >= 4:
                        tn.wait_ge(sm, i - 3)
                    pb = p_o[:, (i % 4) * 512:(i % 4) * 512 + 512]
                    for c in range(ND):
                        ins = tn.matmul(pb,
                                        K.fm2[:, c * 2048 + t * 128: c * 2048 + (t + 1) * 128],
                                        K.wbuf[:, c * 1024 + dg * 512: c * 1024 + (dg + 1) * 512],
                                        start=(c == 0), stop=(c == ND - 1))
                    ins.then_inc(mmo, 1)

            @b.scalar
            def _(s):
                for i in range(NT * 2):
                    t, dg = i // 2, i % 2
                    if i >= 4:
                        s.wait_ge(sm, i - 3)
                    s.activation(K.scr[:, (i % 4) * 512:(i % 4) * 512 + 512],
                                 K.resid[:, t * 1024 + dg * 512: t * 1024 + (dg + 1) * 512],
                                 AF.Identity,
                                 bias=st[:, ST_SV_NMR + t:ST_SV_NMR + t + 1],
                                 scale=st[:, ST_SV_R + t:ST_SV_R + t + 1]).then_inc(zs, 1)

            @b.vector
            def _(v):
                for i in range(NT * 2):
                    t, dg = i // 2, i % 2
                    v.wait_ge(mmo, i + 1)
                    v.wait_ge(zs, i + 1)
                    v.scalar_tensor_tensor(K.fm1[:, t * 1024 + dg * 512: t * 1024 + (dg + 1) * 512],
                                           p_o[:, (i % 4) * 512:(i % 4) * 512 + 512],
                                           1.0, K.scr[:, (i % 4) * 512:(i % 4) * 512 + 512],
                                           ALU.mult, ALU.add).then_inc(sm, 1)


def emit_l0ffn_gate(K, tag, wg, dgoff):
    tag = K.tag(tag)
    nc = K.nc
    K.lw_n += 16
    lw_tgt = K.lw_n
    sc_all = K.misc[:, 512:640]
    with ExitStack() as ps:
        e = ps.enter_context
        p_g = e(nc.psum_tensor(f"pgg_{tag}", [128, 4 * 512], F32))
        mmg = e(nc.semaphore(f"mmg_{tag}"))
        evg = e(nc.semaphore(f"evg_{tag}"))
        nmd = e(nc.semaphore(f"nmd_{tag}"))
        exd = e(nc.semaphore(f"exd_{tag}"))
        with nc.Block(f"wg_{tag}") as b:
            @b.gpsimd
            def _(g):
                g.dma_start(out=K.wbuf[:, 0:64].rearrange("p (c m) -> p c m", c=ND),
                            in_=wg.rearrange("(c p) m -> p c m", p=128)).then_inc(K.lw, 16)

            @b.tensor
            def _(tn):
                tn.wait_ge(K.lw, lw_tgt)
                for t in range(NT):
                    if t >= 4:
                        tn.wait_ge(evg, t - 3)
                    pb = p_g[:, (t % 4) * 512:(t % 4) * 512 + 8]
                    for kc in range(ND):
                        ins = tn.matmul(pb,
                                        K.fm2[:, kc * 2048 + t * 128: kc * 2048 + (t + 1) * 128],
                                        K.wbuf[:, kc * 8:(kc + 1) * 8],
                                        start=(kc == 0), stop=(kc == ND - 1))
                    ins.then_inc(mmg, 1)

            @b.vector
            def _(v):
                mxb = K.stats[:, ST_T0:ST_T0 + 16]
                nmx = K.stats[:, ST_T1:ST_T1 + 16]
                sums = K.stats[:, ST_SUMS:ST_SUMS + 16]
                gates = K.stats[:, ST_M2:ST_M2 + 16]
                msk = K.scr
                for t in range(NT):
                    v.wait_ge(mmg, t + 1)
                    v.tensor_copy(sc_all[:, t * 8:(t + 1) * 8],
                                  p_g[:, (t % 4) * 512:(t % 4) * 512 + 8]).then_inc(evg, 1)
                for tq in range(0, NT, 4):
                    for i in range(4):
                        t = tq + i
                        v.reduce_max(mxb[:, t:t + 1].unsqueeze(2),
                                     sc_all[:, t * 8:(t + 1) * 8].unsqueeze(1), axis=AX.X)
                    for i in range(4):
                        t = tq + i
                        ins = v.tensor_scalar(nmx[:, t:t + 1], mxb[:, t:t + 1], -1.0, None, ALU.mult)
                    ins.then_inc(nmd, 1)
                    v.wait_ge(exd, tq // 4 + 1)
                    v.reciprocal(gates[:, tq:tq + 4], sums[:, tq:tq + 4])
                    for i in range(4):
                        t = tq + i
                        v.tensor_single_scalar(msk[:, i * 8:(i + 1) * 8], sc_all[:, t * 8:(t + 1) * 8],
                                               mxb[:, t:t + 1], ALU.is_ge)
                    for i in range(4):
                        t = tq + i
                        v.tensor_single_scalar(K.pvec[:, dgoff + t * 8: dgoff + (t + 1) * 8],
                                               msk[:, i * 8:(i + 1) * 8],
                                               gates[:, t:t + 1], ALU.mult)

            @b.scalar
            def _(s):
                sums = K.stats[:, ST_SUMS:ST_SUMS + 16]
                nmx = K.stats[:, ST_T1:ST_T1 + 16]
                for tq in range(0, NT, 4):
                    s.wait_ge(nmd, tq // 4 + 1)
                    for i in range(4):
                        t = tq + i
                        ins = s.activation(K.scr[:, 1024 + i * 8:1024 + (i + 1) * 8],
                                           sc_all[:, t * 8:(t + 1) * 8], AF.Exp,
                                           bias=nmx[:, t:t + 1],
                                           accum_out=sums[:, t:t + 1])
                    ins.then_inc(exd, 1)


def emit_l0ffn_moe(K, tag, we1, we2, be2rows, be1_off, dgoff):
    tag = K.tag(tag)
    nc = K.nc
    frag = ([K.qk[:, c * 512:(c + 1) * 512] for c in range(16)]
            + [K.vt[:, c * 512:(c + 1) * 512] for c in range(8)]
            + [K.expt[:, c * 512:(c + 1) * 512] for c in range(5)]
            + [K.xtok[:, c * 512:(c + 1) * 512] for c in range(3)])
    assert len(frag) == 32
    moeacc = K.scr[:, :].bitcast(BF16)
    n_w = 4 * 8 * 13
    K.lw_n += n_w * 16
    lw_base = K.lw_n - n_w * 16

    def lw_cnt(tc, ex, j):
        return (tc * 8 + ex) * 13 + j + 1

    with ExitStack() as ps:
        e = ps.enter_context
        p_h = e(nc.psum_tensor(f"ph_{tag}", [128, 4 * 512], F32))
        p_y = e(nc.psum_tensor(f"py_{tag}", [128, 4 * 512], F32))
        mmh = e(nc.semaphore(f"mmh_{tag}"))
        hev = e(nc.semaphore(f"hev_{tag}"))
        mmy = e(nc.semaphore(f"mmy_{tag}"))
        yac = e(nc.semaphore(f"yac_{tag}"))
        msd = e(nc.semaphore(f"msd_{tag}"))
        yfin = e(nc.semaphore(f"yfin_{tag}"))
        cst = e(nc.semaphore(f"cst_{tag}"))
        gsc = e(nc.semaphore(f"gsc_{tag}"))
        n_cache = [0]

        with nc.Block(f"moe_{tag}") as b:
            NTC, NEX = DBG_MOE if DBG_MOE else (4, 8)

            @b.gpsimd
            def _(g):
                w1n = 0
                for tc in range(NTC):
                    for ex in range(NEX):
                        w1_fp32, w1_c, use_c = we1[ex]
                        w2_fp32, w2_c, _ = we2[ex]
                        conv = (tc == 0) and (not use_c) and not globals().get('DBG_NOCONV')
                        for ocg in range(8):
                            if w1n >= 2:
                                g.wait_ge(mmh, (w1n - 1) * 4)
                            if ocg < 2 and (tc + ex) > 0 and not globals().get('DBG_MOE_W1'):
                                g.wait_ge(mmy, (tc * 8 + ex) * 16)
                            src = w1_c if (use_c or tc > 0) else w1_fp32
                            g.dma_start(out=K.wbuf[:, (w1n % 2) * 4096:(w1n % 2) * 4096 + 4096]
                                        .rearrange("p (c m) -> p c m", c=ND),
                                        in_=src[:, ocg * 512:(ocg + 1) * 512]
                                        .rearrange("(c p) m -> p c m", p=128)).then_inc(K.lw, 16)
                            if conv:
                                g.wait_ge(K.lw, lw_base + 16 * lw_cnt(tc, ex, ocg))
                                g.dma_start(out=w1_c[:, ocg * 512:(ocg + 1) * 512]
                                            .rearrange("(c p) m -> p c m", p=128),
                                            in_=K.wbuf[:, (w1n % 2) * 4096:(w1n % 2) * 4096 + 4096]
                                            .rearrange("p (c m) -> p c m", c=ND)).then_inc(cst, 16)
                                n_cache[0] += 1
                            w1n += 1
                        if globals().get('DBG_MOE_W1'):
                            continue
                        if tc * 8 + ex > 0:
                            g.wait_ge(mmy, (tc * 8 + ex) * 16)
                        g.dma_start(out=K.brow[0:1, 0:1024], in_=be2rows[ex]).then_inc(K.lw, 16)
                        for dg in range(4):
                            g.wait_ge(mmh, w1n * 4)
                            if dg == 0:
                                if tc * 8 + ex > 0:
                                    g.wait_ge(mmy, (tc * 8 + ex) * 16)
                            else:
                                g.wait_ge(mmy, (tc * 8 + ex) * 16 + dg * 4)
                            src = w2_c if (use_c or tc > 0) else w2_fp32
                            g.dma_start(out=K.wbuf[:, 0:8192].rearrange("p (c m) -> p c m", c=32),
                                        in_=src[:, dg * 256:(dg + 1) * 256]
                                        .rearrange("(c p) m -> p c m", p=128)).then_inc(K.lw, 16)
                            if conv:
                                g.wait_ge(K.lw, lw_base + 16 * lw_cnt(tc, ex, 9 + dg))
                                g.dma_start(out=w2_c[:, dg * 256:(dg + 1) * 256]
                                            .rearrange("(c p) m -> p c m", p=128),
                                            in_=K.wbuf[:, 0:8192].rearrange("p (c m) -> p c m", c=32)
                                            ).then_inc(cst, 16)
                                n_cache[0] += 1
                g.wait_ge(cst, 16 * n_cache[0])

            @b.vector
            def _(v):
                ny = 0
                for tc in range(NTC):
                    if tc >= 1:
                        v.wait_ge(yfin, tc * 4)
                    if not globals().get('DBG_MOE_NOMEMSET'):
                        for ttm in range(4):
                            v.memset(K.scr[:, ttm * 512:(ttm + 1) * 512], 0.0)
                    v.sem_inc(msd, 1)
                    if globals().get('DBG_MOE_W1'):
                        continue
                    for ex in range(NEX):
                        for dg in range(4):
                            for tt in range(4):
                                v.wait_ge(gsc, ny + 1)
                                sl = moeacc[:, tt * 1024 + dg * 256: tt * 1024 + (dg + 1) * 256]
                                if globals().get('DBG_MOE_G'):
                                    v.sem_inc(yac, 1)
                                    ny += 1
                                    continue
                                v.tensor_add(sl, sl,
                                             K.misc[:, (ny % 4) * 256:(ny % 4) * 256 + 256]
                                             ).then_inc(yac, 1)
                                ny += 1
                    for tt in range(4):
                        if globals().get('DBG_MOE_NOSTT'):
                            v.sem_inc(yfin, 1)
                            continue
                        t = tc * 4 + tt
                        v.tensor_add(K.fm1[:, t * 1024:(t + 1) * 1024],
                                     K.fm1[:, t * 1024:(t + 1) * 1024],
                                     moeacc[:, tt * 1024:(tt + 1) * 1024]).then_inc(yfin, 1)

            @b.tensor
            def _(tn):
                nm = 0
                ny = 0
                w1n = 0
                for tc in range(NTC):
                    tn.wait_ge(msd, tc + 1)
                    for ex in range(NEX):
                        for ocg in range(8):
                            tn.wait_ge(K.lw, lw_base + 16 * lw_cnt(tc, ex, ocg))
                            for f4 in range(4):
                                if nm >= 4:
                                    tn.wait_ge(hev, nm - 3)
                                pb = p_h[:, (nm % 4) * 512:(nm % 4) * 512 + 512]
                                woff = (w1n % 2) * 4096
                                for kc in range(ND):
                                    ins = tn.matmul(pb,
                                                    K.wbuf[:, woff + kc * 512 + f4 * 128: woff + kc * 512 + (f4 + 1) * 128],
                                                    K.fm2[:, kc * 2048 + tc * 512: kc * 2048 + (tc + 1) * 512],
                                                    start=(kc == 0), stop=(kc == ND - 1))
                                ins.then_inc(mmh, 1)
                                nm += 1
                            w1n += 1
                        if globals().get('DBG_MOE_W1'):
                            continue
                        for dg in range(4):
                            tn.wait_ge(K.lw, lw_base + 16 * lw_cnt(tc, ex, 9 + dg))
                            tn.wait_ge(hev, (tc * 8 + ex) * 32 + 32)
                            for tt in range(4):
                                if ny >= 4:
                                    tn.wait_ge(gsc, ny - 3)
                                pb = p_y[:, (ny % 4) * 512:(ny % 4) * 512 + 256]
                                for ffc in range(32):
                                    tn.matmul(pb,
                                              frag[ffc][:, tt * 128:(tt + 1) * 128],
                                              K.wbuf[:, ffc * 256:(ffc + 1) * 256],
                                              start=(ffc == 0), stop=False)
                                tn.matmul(pb, K.ones_rb[0:1, 0:128],
                                          K.brow[0:1, dg * 256:(dg + 1) * 256],
                                          start=False, stop=True).then_inc(mmy, 1)
                                ny += 1

            @b.scalar
            def _(s):
                nm = 0
                ng = 0
                for tc in range(NTC):
                    for ex in range(NEX):
                        prev = tc * 8 + ex - 1
                        for ocg in range(8):
                            for f4 in range(4):
                                ffc = ocg * 4 + f4
                                s.wait_ge(mmh, nm + 1)
                                if prev >= 0 and not globals().get('DBG_MOE_W1'):
                                    s.wait_ge(mmy, (prev + 1) * 16)
                                if globals().get('DBG_MOE_NORELU'):
                                    s.copy(frag[ffc], p_h[:, (nm % 4) * 512:(nm % 4) * 512 + 512]
                                           ).then_inc(hev, 1)
                                else:
                                    s.activation(frag[ffc], p_h[:, (nm % 4) * 512:(nm % 4) * 512 + 512],
                                                 AF.Relu,
                                                 bias=K.pvec[:, be1_off + ex * 32 + ffc: be1_off + ex * 32 + ffc + 1]
                                                 ).then_inc(hev, 1)
                                nm += 1
                        if globals().get('DBG_MOE_W1'):
                            continue
                        for dg in range(4):
                            for tt in range(4):
                                t = tc * 4 + tt
                                s.wait_ge(mmy, ng + 1)
                                if ng >= 4:
                                    s.wait_ge(yac, ng - 3)
                                if globals().get('DBG_MOE_G') == 2:
                                    s.sem_inc(gsc, 1)
                                    ng += 1
                                    continue
                                s.activation(K.misc[:, (ng % 4) * 256:(ng % 4) * 256 + 256],
                                             p_y[:, (ng % 4) * 512:(ng % 4) * 512 + 256],
                                             AF.Copy,
                                             scale=K.pvec[:, dgoff + t * 8 + ex: dgoff + t * 8 + ex + 1]
                                             ).then_inc(gsc, 1)
                                ng += 1


def emit_l0_ffn(K, w, call_idx):
    emit_ln(K, "0ff", save=True)
    K.barrier()
    if DBG_L0F == "ln":
        return
    if call_idx == 0:
        emit_l0ffn_rprep(K, w["perev"], w["wr"])
        K.barrier()
    if DBG_L0F == "rprep":
        return
    if call_idx == 0:
        with K.nc.Block("dvprep") as b:
            @b.vector
            def _(v):
                v.tensor_sub(K.pvec[:, PV["l0f_dv"]:PV["l0f_dv"] + 8],
                             K.pvec[:, PV["l0f_v"]:PV["l0f_v"] + 8],
                             K.pvec[:, PV["l0f_u"]:PV["l0f_u"] + 8])
        K.barrier()
    for qtr in range(4):
        emit_qkv_quarter(K, "0ff", qtr, w["wq"][qtr], w["wk"][qtr], w["wv"][qtr],
                         bq_off=PV["l0f_u"] + 2 * qtr)
        K.barrier()
        if DBG_L0F == f"qkv{qtr}":
            return
        emit_l0ffn_attn_quarter(K, "0ffa", qtr, dvoff=PV["l0f_dv"])
        K.barrier()
        if DBG_L0F == f"attn{qtr}":
            return
    emit_l0ffn_y(K, "0ffy", w["wo"])
    K.barrier()
    if DBG_L0F == "y":
        return
    emit_ln_tok(K, "0ffl1", w["gb1"], "store")
    K.barrier()
    if DBG_L0F == "ln1":
        return
    emit_l1ffn_ffn(K, "0ffn", w["wff1"], w["wff2"], PV["l0f_bff1"], w["bff2row"])
    K.barrier()
    if DBG_L0F == "ffn":
        return
    emit_ln_tok(K, "0ffl2", w["gb2"], "store")
    K.barrier()
    if DBG_L0F == "ln2":
        return
    emit_l0ffn_gate(K, "0ffg", w["wg"], PV["l0f_dg"])
    K.barrier()
    if DBG_L0F == "gate":
        return
    use_c = call_idx > 0
    we1 = [(w["we1"][ex], w["we1c"][ex], use_c) for ex in range(8)]
    we2 = [(w["we2"][ex], w["we2c"][ex], use_c) for ex in range(8)]
    emit_l0ffn_moe(K, "0ffm", we1, we2, [w["be2"][ex] for ex in range(8)],
                   PV["l0f_be1"], PV["l0f_dg"])
    K.barrier()
    if DBG_L0F == "moe":
        return
    emit_ln_tok(K, "0ffl3", w["gb3"], "store")
    K.barrier()
    emit_proj_add(K, "0ffo", w["wout"], None, add_z=False)
    K.barrier()


def emit_attn_l1aa_quarter(K, tag, qtr):
    tag = K.tag(tag)
    nc = K.nc
    scale = 0.125
    with ExitStack() as ps:
        e = ps.enter_context
        p_sc = e(nc.psum_tensor(f"psc_{tag}{qtr}", [128, 2 * 512], F32))
        p_av = e(nc.psum_tensor(f"pav_{tag}{qtr}", [128, 2 * 512], F32))
        p_rb = e(nc.psum_tensor(f"prb_{tag}{qtr}", [128, 2 * 512], F32))
        scm = e(nc.semaphore(f"scm_{tag}{qtr}"))
        exd = e(nc.semaphore(f"exd_{tag}{qtr}"))
        gsd = e(nc.semaphore(f"gsd_{tag}{qtr}"))
        avm = e(nc.semaphore(f"avm_{tag}{qtr}"))
        rcv = e(nc.semaphore(f"rcv_{tag}{qtr}"))
        rbm = e(nc.semaphore(f"rbm_{tag}{qtr}"))
        rbe = e(nc.semaphore(f"rbe_{tag}{qtr}"))
        ctd = e(nc.semaphore(f"ctd_{tag}{qtr}"))
        gad = e(nc.semaphore(f"gad_{tag}{qtr}"))

        iters = []
        for hl in range(4):
            for qc in range(4):
                iters.append((hl, qc, "L"))
                iters.append((hl, qc, "G"))

        def kcs_of(qc, mode):
            if mode == "G":
                return [0, 1], []
            w, ql = qc // 2, qc % 2
            kcs = [8 * w + j for j in range(4 * ql + 4)]
            diag = [kc for kc in kcs if kc >= 8 * w + 4 * ql]
            return kcs, diag

        with nc.Block(f"attnw_{tag}{qtr}") as b:
            @b.tensor
            def _(tn):
                nsc = 0
                for it, (hl, qc, mode) in enumerate(iters):
                    P = hl // 2
                    r0 = 64 * (hl % 2)
                    kcs, diag = kcs_of(qc, mode)
                    if it >= 1:
                        tn.wait_ge(rcv, it)
                        if it >= 3:
                            tn.wait_ge(rbe, it - 2)
                        tn.matmul(p_rb[0:64, ((it - 1) % 2) * 512:((it - 1) % 2) * 512 + 512],
                                  K.ones_rf[0:1, 0:64],
                                  K.rowf[0:1, ((it - 1) % 2) * 512:((it - 1) % 2) * 512 + 512],
                                  start=True, stop=True).then_inc(rbm, 1)
                    if it >= 2:
                        tn.wait_ge(ctd, it - 1)
                    for j, kc in enumerate(kcs):
                        if nsc >= 2:
                            tn.wait_ge(exd, nsc - 1)
                        tn.matmul(p_sc[:, (nsc % 2) * 512:(nsc % 2) * 512 + 512],
                                  K.qk[r0:r0 + 64, 4096 + P * 2048 + kc * 128: 4096 + P * 2048 + (kc + 1) * 128],
                                  K.qk[r0:r0 + 64, P * 2048 + qc * 512: P * 2048 + (qc + 1) * 512],
                                  start=True, stop=True).then_inc(scm, 1)
                        tn.wait_ge(exd, nsc + 1)
                        ins = tn.matmul(p_av[0:65, (it % 2) * 512:(it % 2) * 512 + 512],
                                        K.vt[:, (kc * 4 + hl) * 65:(kc * 4 + hl) * 65 + 65],
                                        K.expt[:, (nsc % 3) * 512:(nsc % 3) * 512 + 512],
                                        start=(j == 0), stop=(j == len(kcs) - 1))
                        nsc += 1
                    ins.then_inc(avm, 1)
                it = len(iters)
                tn.wait_ge(rcv, it)
                tn.wait_ge(rbe, it - 2)
                tn.matmul(p_rb[0:64, ((it - 1) % 2) * 512:((it - 1) % 2) * 512 + 512],
                          K.ones_rf[0:1, 0:64],
                          K.rowf[0:1, ((it - 1) % 2) * 512:((it - 1) % 2) * 512 + 512],
                          start=True, stop=True).then_inc(rbm, 1)

            @b.scalar
            def _(s):
                nsc = 0
                for it, (hl, qc, mode) in enumerate(iters):
                    kcs, diag = kcs_of(qc, mode)
                    for j, kc in enumerate(kcs):
                        s.wait_ge(scm, nsc + 1)
                        is_diag = kc in diag
                        s.activation(K.expt[:, (nsc % 3) * 512:(nsc % 3) * 512 + 512],
                                     p_sc[:, (nsc % 2) * 512:(nsc % 2) * 512 + 512],
                                     AF.Exp, scale=scale).then_inc(gsd if is_diag else exd, 1)
                        nsc += 1
                    s.wait_ge(rbm, it + 1)
                    if it >= 1:
                        s.wait_ge(ctd, it)
                    s.copy(K.rb[0:64, :], p_rb[0:64, (it % 2) * 512:(it % 2) * 512 + 512]).then_inc(rbe, 1)

            @b.gpsimd
            def _(g):
                nsc = 0
                ng = 0
                for it, (hl, qc, mode) in enumerate(iters):
                    kcs, diag = kcs_of(qc, mode)
                    for j, kc in enumerate(kcs):
                        if kc in diag:
                            ng += 1
                            g.wait_ge(gsd, ng)
                            g.affine_select(K.expt[:, (nsc % 3) * 512:(nsc % 3) * 512 + 512],
                                            K.expt[:, (nsc % 3) * 512:(nsc % 3) * 512 + 512],
                                            [[1, 512]], ALU.is_ge, 0.0,
                                            base=qc * 512 - kc * 128,
                                            channel_multiplier=-1).then_inc(exd, 1)
                        nsc += 1
                    if mode == "G":
                        g.wait_ge(ctd, it + 1)
                        P = hl // 2
                        r0 = 64 * (hl % 2)
                        gp = qtr * 2 + P
                        g.tensor_add(K.fm2[r0:r0 + 64, gp * 2048 + qc * 512: gp * 2048 + (qc + 1) * 512],
                                     K.fm2[r0:r0 + 64, gp * 2048 + qc * 512: gp * 2048 + (qc + 1) * 512],
                                     K.scr[r0:r0 + 64, 0:512]).then_inc(gad, 1)

            @b.vector
            def _(v):
                for it, (hl, qc, mode) in enumerate(iters):
                    P = hl // 2
                    r0 = 64 * (hl % 2)
                    gp = qtr * 2 + P
                    v.wait_ge(avm, it + 1)
                    v.reciprocal(K.rowf[0:1, (it % 2) * 512:(it % 2) * 512 + 512],
                                 p_av[64:65, (it % 2) * 512:(it % 2) * 512 + 512]).then_inc(rcv, 1)
                    v.wait_ge(rbe, it + 1)
                    if mode == "L":
                        v.tensor_mul(K.fm2[r0:r0 + 64, gp * 2048 + qc * 512: gp * 2048 + (qc + 1) * 512],
                                     p_av[0:64, (it % 2) * 512:(it % 2) * 512 + 512],
                                     K.rb[0:64, :]).then_inc(ctd, 1)
                    else:
                        if it >= 2:
                            v.wait_ge(gad, it // 2)
                        v.tensor_mul(K.scr[r0:r0 + 64, 0:512],
                                     p_av[0:64, (it % 2) * 512:(it % 2) * 512 + 512],
                                     K.rb[0:64, :]).then_inc(ctd, 1)


def emit_l1_attn_a(K, w):
    emit_ln(K, "1aa")
    K.barrier()
    for qtr in range(4):
        emit_qkv_quarter(K, "1aa", qtr, w["wq"][qtr], w["wk"][qtr], w["wv"][qtr])
        K.barrier()
        emit_attn_l1aa_quarter(K, "1aa", qtr)
        K.barrier()
    emit_proj_add(K, "1aa", w["wo"], None, add_z=False)
    K.barrier()


def emit_decay_prep(K, decay_raw):
    nc = K.nc
    K.ld_n += 16
    ld_tgt = K.ld_n
    st = K.stats
    with ExitStack() as ps:
        e = ps.enter_context
        p_b = e(nc.psum_tensor("pdec", [128, 512], F32))
        s1 = e(nc.semaphore("dc1"))
        s2 = e(nc.semaphore("dc2"))
        s3 = e(nc.semaphore("dc3"))
        raw = K.rowf[0:1, 0:16]
        a_abs = K.rowf[0:1, 16:32]
        mx = K.rowf[0:1, 32:48]
        lnp = K.rowf[0:1, 48:64]
        ep1 = K.rowf[0:1, 80:96]
        sp = K.rowf[0:1, 64:80]
        with nc.Block("decprep") as b:
            @b.sync
            def _(sy):
                sy.dma_start(out=raw, in_=decay_raw).then_inc(K.ld, 16)

            @b.vector
            def _(v):
                v.wait_ge(s2, 1)
                v.tensor_single_scalar(ep1, lnp, 1.0, ALU.add).then_inc(s1, 1)

            @b.scalar
            def _(s):
                s.wait_ge(K.ld, ld_tgt)
                s.activation(lnp, raw, AF.Exp).then_inc(s2, 1)
                s.wait_ge(s1, 1)
                s.activation(sp, ep1, AF.Ln).then_inc(s1, 1)
                s.wait_ge(s3, 1)
                s.copy(st[:, ST_DEC:ST_DEC + 16], p_b[:, 0:16]).then_inc(s3, 1)

            @b.tensor
            def _(tn):
                tn.wait_ge(s1, 2)
                tn.matmul(p_b[:, 0:16], K.ones_rf[0:1, 0:128], sp,
                          start=True, stop=True).then_inc(s3, 1)


def emit_attn_decay_quarter(K, tag, qtr, scale, l1p, no_mask=False):
    tag = K.tag(tag)
    nc = K.nc
    st = K.stats
    with ExitStack() as ps:
        e = ps.enter_context
        p_sc = e(nc.psum_tensor(f"psc_{tag}{qtr}", [128, 2 * 512], F32))
        p_av = e(nc.psum_tensor(f"pav_{tag}{qtr}", [128, 4 * 512], F32))
        p_rb = e(nc.psum_tensor(f"prb_{tag}{qtr}", [128, 2 * 512], F32))
        tld = e(nc.semaphore(f"tld_{tag}{qtr}"))
        scm = e(nc.semaphore(f"scm_{tag}{qtr}"))
        lgd = e(nc.semaphore(f"lgd_{tag}{qtr}"))
        eid = e(nc.semaphore(f"eid_{tag}{qtr}"))
        exd = e(nc.semaphore(f"exd_{tag}{qtr}"))
        avm = e(nc.semaphore(f"avm_{tag}{qtr}"))
        rcv = e(nc.semaphore(f"rcv_{tag}{qtr}"))
        rbm = e(nc.semaphore(f"rbm_{tag}{qtr}"))
        rbe = e(nc.semaphore(f"rbe_{tag}{qtr}"))
        ctd = e(nc.semaphore(f"ctd_{tag}{qtr}"))
        n_q = 4

        def kcs_of(qc):
            return list(range(4 * qc + 4))

        with nc.Block(f"attnd_{tag}{qtr}") as b:
            ld_base = [0]

            @b.sync
            def _(sy):
                if no_mask:
                    return
                n = 0
                for qc in range(n_q):
                    for j, kc in enumerate(kcs_of(qc)):
                        if n >= 2:
                            sy.wait_ge(lgd, 4 * (n - 1))
                        sy.dma_start(out=K.scr[:, (n % 2) * 512:(n % 2) * 512 + 512],
                                     in_=l1p[4 * qc - kc + 3]).then_inc(tld, 16)
                        n += 1

            @b.scalar
            def _(s):
                n = 0
                nex = 0
                for qc in range(n_q):
                    for j, kc in enumerate(kcs_of(qc)):
                        if not no_mask:
                            s.wait_ge(tld, 16 * (n + 1))
                        for hl in range(4):
                            lslot = 1024 + ((4 * n + hl) % 2) * 512
                            if not no_mask:
                                if 4 * n + hl >= 2:
                                    s.wait_ge(eid, 4 * n + hl - 1)
                                s.activation(K.scr[:, lslot:lslot + 512],
                                             K.scr[:, (n % 2) * 512:(n % 2) * 512 + 512],
                                             AF.Ln, bias=st[:, ST_ONE:ST_ONE + 1],
                                             scale=st[:, ST_DEC + 4 * qtr + hl:ST_DEC + 4 * qtr + hl + 1]
                                             ).then_inc(lgd, 1)
                            if no_mask:
                                s.wait_ge(scm, 4 * n + hl + 1)
                                s.activation(K.expt[:, (nex % 3) * 512:(nex % 3) * 512 + 512],
                                             p_sc[:, ((4 * n + hl) % 2) * 512:((4 * n + hl) % 2) * 512 + 512],
                                             AF.Exp, scale=scale).then_inc(exd, 1)
                            else:
                                s.wait_ge(eid, 4 * n + hl + 1)
                                s.activation(K.expt[:, (nex % 3) * 512:(nex % 3) * 512 + 512],
                                             p_sc[:, ((4 * n + hl) % 2) * 512:((4 * n + hl) % 2) * 512 + 512],
                                             AF.Exp).then_inc(exd, 1)
                            nex += 1
                        n += 1
                    for hl in range(4):
                        it = qc * 4 + hl
                        s.wait_ge(rbm, it + 1)
                        if it >= 1:
                            s.wait_ge(ctd, it)
                        s.copy(K.rb[0:64, :], p_rb[0:64, (it % 2) * 512:(it % 2) * 512 + 512]).then_inc(rbe, 1)

            @b.vector
            def _(v):
                nst = 0
                for qc in range(n_q):
                    for j, kc in enumerate(kcs_of(qc)):
                        for hl in range(4):
                            if not no_mask:
                                v.wait_ge(scm, nst + 1)
                                v.wait_ge(lgd, nst + 1)
                                bank = (nst % 2) * 512
                                lslot = 1024 + (nst % 2) * 512
                                v.scalar_tensor_tensor(p_sc[:, bank:bank + 512],
                                                       p_sc[:, bank:bank + 512], scale,
                                                       K.scr[:, lslot:lslot + 512],
                                                       ALU.mult, ALU.subtract).then_inc(eid, 1)
                            nst += 1
                    for hl in range(4):
                        it = qc * 4 + hl
                        v.wait_ge(avm, it + 1)
                        v.reciprocal(K.rowf[0:1, (it % 2) * 512:(it % 2) * 512 + 512],
                                     p_av[64:65, hl * 512:hl * 512 + 512]).then_inc(rcv, 1)
                        v.wait_ge(rbe, it + 1)
                        P = hl // 2
                        r0 = 64 * (hl % 2)
                        gp = qtr * 2 + P
                        v.tensor_mul(K.fm2[r0:r0 + 64, gp * 2048 + qc * 512: gp * 2048 + (qc + 1) * 512],
                                     p_av[0:64, hl * 512:hl * 512 + 512],
                                     K.rb[0:64, :]).then_inc(ctd, 1)

            @b.tensor
            def _(tn):
                nst = 0
                for qc in range(n_q):
                    kcs = kcs_of(qc)
                    if qc >= 1:
                        tn.wait_ge(ctd, 4 * qc)
                    for j, kc in enumerate(kcs):
                        for hl in range(4):
                            P = hl // 2
                            r0 = 64 * (hl % 2)
                            if nst >= 2:
                                tn.wait_ge(exd, nst - 1)
                            tn.matmul(p_sc[:, (nst % 2) * 512:(nst % 2) * 512 + 512],
                                      K.qk[r0:r0 + 64, 4096 + P * 2048 + kc * 128: 4096 + P * 2048 + (kc + 1) * 128],
                                      K.qk[r0:r0 + 64, P * 2048 + qc * 512: P * 2048 + (qc + 1) * 512],
                                      start=True, stop=True).then_inc(scm, 1)
                            tn.wait_ge(exd, nst + 1)
                            ins = tn.matmul(p_av[0:65, hl * 512:hl * 512 + 512],
                                            K.vt[:, (kc * 4 + hl) * 65:(kc * 4 + hl) * 65 + 65],
                                            K.expt[:, (nst % 3) * 512:(nst % 3) * 512 + 512],
                                            start=(j == 0), stop=(j == len(kcs) - 1))
                            if j == len(kcs) - 1:
                                ins.then_inc(avm, 1)
                            nst += 1
                    for hl in range(4):
                        it = qc * 4 + hl
                        tn.wait_ge(rcv, it + 1)
                        if it >= 2:
                            tn.wait_ge(rbe, it - 1)
                        tn.matmul(p_rb[0:64, (it % 2) * 512:(it % 2) * 512 + 512],
                                  K.ones_rf[0:1, 0:64],
                                  K.rowf[0:1, (it % 2) * 512:(it % 2) * 512 + 512],
                                  start=True, stop=True).then_inc(rbm, 1)


def emit_l1_attn_b(K, w):
    emit_ln(K, "1ab")
    K.barrier()
    emit_decay_prep(K, w["decay"])
    K.barrier()
    if DBG_STAGE == 1:
        return
    for qtr in range(4):
        emit_qkv_quarter(K, "1ab", qtr, w["wq"][qtr], w["wk"][qtr], w["wv"][qtr])
        K.barrier()
        emit_attn_decay_quarter(K, "1ab", qtr, scale=0.125, l1p=w["l1p"], no_mask=DBG_NOMASK)
        K.barrier()
        if DBG_STAGE == 2:
            return
    emit_proj_add(K, "1ab", w["wproj"], None, add_z=False)
    K.barrier()



def emit_gate(K, tag, wsel):
    tag = K.tag(tag)
    nc = K.nc
    K.lw_n += 16
    lw_tgt = K.lw_n
    BIG = 1e30
    with ExitStack() as ps:
        e = ps.enter_context
        p_g = e(nc.psum_tensor(f"pg_{tag}", [128, 4 * 512], F32))
        mmg = e(nc.semaphore(f"mmg_{tag}"))
        sgd = e(nc.semaphore(f"sgd_{tag}"))
        mkd = e(nc.semaphore(f"mkd_{tag}"))
        gs = K.misc[:, 512:512 + 512]
        with nc.Block(f"gate_{tag}") as b:
            @b.gpsimd
            def _(g):
                g.dma_start(out=K.wbuf[:, 0:1024].rearrange("p (c m) -> p c m", c=ND),
                            in_=wsel.rearrange("(c p) m -> p c m", p=128)).then_inc(K.lw, 16)

            @b.tensor
            def _(tn):
                tn.wait_ge(K.lw, lw_tgt)
                for t in range(NT):
                    if t >= 4:
                        tn.wait_ge(sgd, t - 3)
                    pb = p_g[:, (t % 4) * 512:(t % 4) * 512 + 128]
                    for kc in range(ND):
                        ins = tn.matmul(pb,
                                        K.fm1[:, kc * 2048 + t * 128: kc * 2048 + (t + 1) * 128],
                                        K.wbuf[:, kc * 128:(kc + 1) * 128],
                                        start=(kc == 0), stop=(kc == ND - 1))
                    ins.then_inc(mmg, 1)

            @b.scalar
            def _(s):
                for t in range(NT):
                    s.wait_ge(mmg, t + 1)
                    if t >= 4:
                        s.wait_ge(mkd, t - 3)
                    s.activation(gs[:, (t % 4) * 128:(t % 4) * 128 + 128],
                                 p_g[:, (t % 4) * 512:(t % 4) * 512 + 128],
                                 AF.Sigmoid).then_inc(sgd, 1)

            @b.vector
            def _(v):
                tmp = K.scr
                for tq in range(0, NT, 4):
                    quad = [tq, tq + 1, tq + 2, tq + 3]
                    v.wait_ge(sgd, tq + 4)
                    g2 = [gs[:, (t % 4) * 128:(t % 4) * 128 + 128] for t in quad]
                    g3 = [z.rearrange("p (h e) -> p h e", e=8) for z in g2]
                    mx = [K.stats[:, ST_T0 + 16 * 0:ST_T0 + 16 * 0 + 16],
                          K.stats[:, ST_T0 + 16 * 1:ST_T0 + 16 * 1 + 16],
                          K.stats[:, ST_DEC:ST_DEC + 16],
                          K.stats[:, ST_SUMS:ST_SUMS + 16]]
                    t128 = [tmp[:, i * 128:(i + 1) * 128] for i in range(8)]
                    for i in range(4):
                        v.reduce_max(mx[i].unsqueeze(2), g3[i], axis=AX.X)
                    for i in range(4):
                        v.tensor_tensor(t128[i].rearrange("p (h e) -> p h e", e=8),
                                        g3[i], mx[i].unsqueeze(2).to_broadcast((128, 16, 8)),
                                        ALU.is_ge)
                    for i in range(4):
                        v.scalar_tensor_tensor(t128[i], t128[i], -BIG, g2[i], ALU.mult, ALU.add)
                    for i in range(4):
                        v.reduce_max(mx[i].unsqueeze(2),
                                     t128[i].rearrange("p (h e) -> p h e", e=8), axis=AX.X)
                    for i in range(4):
                        v.tensor_tensor(t128[4 + i].rearrange("p (h e) -> p h e", e=8),
                                        g3[i], mx[i].unsqueeze(2).to_broadcast((128, 16, 8)),
                                        ALU.is_ge)
                    for i, t in enumerate(quad):
                        v.tensor_tensor(K.xtok[:, t * 128:(t + 1) * 128],
                                        g2[i], t128[4 + i], ALU.mult).then_inc(mkd, 1)


def emit_l1ffn_qkv_quarter(K, tag, qtr, wq, wk, wv, ctab, stab):
    tag = K.tag(tag)
    nc = K.nc
    K.lw_n += 6 * 16
    lw_base = K.lw_n - 6 * 16
    with ExitStack() as ps:
        e = ps.enter_context
        p_q = e(nc.psum_tensor(f"pq_{tag}{qtr}", [128, 4 * 512], F32))
        p_t = e(nc.psum_tensor(f"pt_{tag}{qtr}", [128, 2 * 1024], BF16))
        mmq = e(nc.semaphore(f"mmq_{tag}{qtr}"))
        evq = e(nc.semaphore(f"evq_{tag}{qtr}"))
        rpd = e(nc.semaphore(f"rpd_{tag}{qtr}"))
        tpd = e(nc.semaphore(f"tpd_{tag}{qtr}"))
        fmd = e(nc.semaphore(f"fmd_{tag}{qtr}"))
        vgd = e(nc.semaphore(f"vgd_{tag}{qtr}"))
        ld2 = e(nc.semaphore(f"ld2_{tag}{qtr}"))
        stage = K.fm2[:, qtr * 4096: qtr * 4096 + 4096]
        rtab = K.misc[:, 512:1536].bitcast(F32) if False else None

        with nc.Block(f"ffqkv_{tag}{qtr}") as b:
            @b.gpsimd
            def _(g):
                g.dma_start(out=K.wbuf[:, 0:2048].rearrange("p (c m) -> p c m", c=ND),
                            in_=wq.rearrange("(c p) m -> p c m", p=128)).then_inc(K.lw, 16)
                g.dma_start(out=K.wbuf[:, 2048:4096].rearrange("p (c m) -> p c m", c=ND),
                            in_=wk.rearrange("(c p) m -> p c m", p=128)).then_inc(K.lw, 16)
                g.memset(K.vt[:, :].rearrange("p (n o) -> p n o", o=65)[:, :, 64:65], 1.0)
                for hl in range(4):
                    if hl >= 2:
                        g.wait_ge(vgd, 16 * (hl - 1))
                    g.dma_start(out=K.wbuf[:, 4096 + (hl % 2) * 4096: 8192 + (hl % 2) * 4096]
                                .rearrange("p (c m) -> p c m", c=ND),
                                in_=wv[hl].rearrange("(c p) m -> p c m", p=128)).then_inc(K.lw, 16)

            @b.sync
            def _(sy):
                sy.dma_start(out=K.misc[:, 512:768], in_=ctab.rearrange("p (a b) -> p a b", a=4) if False else ctab[:, 0:256]) if False else None
                sy.dma_start(out=K.misc[:, 512:1536], in_=ctab).then_inc(ld2, 16)
                sy.dma_start(out=K.scr[:, 1024:2048], in_=stab).then_inc(ld2, 16)

            @b.tensor
            def _(tn):
                tn.wait_ge(K.lw, lw_base + 32)
                i = 0
                for wi in range(2):
                    for t in range(NT):
                        if i >= 4:
                            tn.wait_ge(evq, i - 3)
                        pb = p_q[:, (i % 4) * 512:(i % 4) * 512 + 256]
                        for kc in range(ND):
                            ins = tn.matmul(pb,
                                            K.fm1[:, kc * 2048 + t * 128: kc * 2048 + (t + 1) * 128],
                                            K.wbuf[:, wi * 2048 + kc * 256: wi * 2048 + (kc + 1) * 256],
                                            start=(kc == 0), stop=(kc == ND - 1))
                        ins.then_inc(mmq, 1)
                        i += 1
                    for t in range(NT):
                        tn.wait_ge(rpd, wi * NT + t + 1)
                        if t >= 2:
                            tn.wait_ge(fmd, wi * NT + t - 1)
                        for P in range(2):
                            ins = tn.matmul(p_t[:, (t % 2) * 1024 + P * 128:(t % 2) * 1024 + (P + 1) * 128],
                                            stage[:, t * 256 + P * 128: t * 256 + (P + 1) * 128],
                                            K.ident[:, :], is_transpose=True,
                                            start=(P == 0), stop=(P == 1))
                        ins.then_inc(tpd, 1)
                iv = 0
                for hl in range(4):
                    tn.wait_ge(K.lw, lw_base + 48 + hl * 16)
                    for t in range(NT):
                        if iv >= 4:
                            tn.wait_ge(vgd, iv - 3)
                        pb = p_q[:, (iv % 4) * 512:(iv % 4) * 512 + 512]
                        for kc in range(ND):
                            ins = tn.matmul(pb,
                                            K.fm1[:, kc * 2048 + t * 128: kc * 2048 + (t + 1) * 128],
                                            K.wbuf[:, 4096 + (hl % 2) * 4096 + kc * 512: 4096 + (hl % 2) * 4096 + (kc + 1) * 512],
                                            start=(kc == 0), stop=(kc == ND - 1))
                        ins.then_inc(mmq, 1)
                        iv += 1

            @b.vector
            def _(v):
                v.wait_ge(ld2, 32)
                i = 0
                for wi in range(2):
                    for t in range(NT):
                        v.wait_ge(mmq, i + 1)
                        v.tensor_copy(stage[:, t * 256:(t + 1) * 256],
                                      p_q[:, (i % 4) * 512:(i % 4) * 512 + 256]).then_inc(evq, 1)
                        i += 1
                    for tp_ in range(0, NT, 2):
                        pair = (tp_, tp_ + 1)
                        views = []
                        for t in pair:
                            s3 = stage[:, t * 256:(t + 1) * 256].rearrange("p (h x) -> p h x", h=4)
                            c3 = K.misc[:, 512 + t * 64:512 + (t + 1) * 64].rearrange("p (h x) -> p h x", h=4)
                            sn3 = K.scr[:, 1024 + t * 64:1024 + (t + 1) * 64].rearrange("p (h x) -> p h x", h=4)
                            views.append((s3[:, :, 0:16], s3[:, :, 16:32], c3, sn3))
                        tv = [K.scr[:, k * 64:(k + 1) * 64].rearrange("p (h x) -> p h x", h=4)
                              for k in range(8)]
                        for i2, (r1, r2, c, sn) in enumerate(views):
                            v.tensor_tensor(tv[4 * i2 + 0], c, r1, ALU.mult)
                            v.tensor_tensor(tv[4 * i2 + 1], sn, r2, ALU.mult)
                            v.tensor_tensor(tv[4 * i2 + 2], sn, r1, ALU.mult)
                            v.tensor_tensor(tv[4 * i2 + 3], c, r2, ALU.mult)
                        for i2, (r1, r2, c, sn) in enumerate(views):
                            v.tensor_tensor(r1, tv[4 * i2 + 0], tv[4 * i2 + 1], ALU.subtract)
                        for i2, (r1, r2, c, sn) in enumerate(views):
                            v.tensor_tensor(r2, tv[4 * i2 + 2], tv[4 * i2 + 3], ALU.add).then_inc(rpd, 1)
                    for t in range(NT):
                        v.wait_ge(tpd, wi * NT + t + 1)
                        src3 = p_t[:, (t % 2) * 1024:(t % 2) * 1024 + 256].rearrange("p (P x) -> p P x", P=2)
                        dst3 = K.qk[:, wi * 4096 + t * 128: wi * 4096 + t * 128 + 2048 + 128]
                        dst3 = bass.AP(K.qk, wi * 4096 + t * 128, [[8192, 128], [2048, 2], [1, 128]])
                        v.tensor_copy(dst3, src3).then_inc(fmd, 1)
                iv = 0
                for quad in range(NT * 4 // 4):
                    idxs = list(range(quad * 4, quad * 4 + 4))
                    infos = []
                    for iv2 in idxs:
                        hl, t = iv2 // NT, iv2 % NT
                        h = qtr * 4 + hl
                        v.wait_ge(mmq, 2 * NT + iv2 + 1)
                        infos.append((iv2, hl, t, h))
                    for (iv2, hl, t, h) in infos:
                        pb = p_q[:, (iv2 % 4) * 512:(iv2 % 4) * 512 + 512]
                        gsl = K.xtok[:, t * 128 + h * 8: t * 128 + h * 8 + 8].unsqueeze(2).to_broadcast((128, 8, 64))
                        v.tensor_tensor(K.scr[:, (iv2 % 4) * 512:(iv2 % 4) * 512 + 512].rearrange("p (e x) -> p e x", e=8),
                                        pb.rearrange("p (e x) -> p e x", e=8), gsl, ALU.mult)
                    for (iv2, hl, t, h) in infos:
                        sc3 = K.scr[:, (iv2 % 4) * 512:(iv2 % 4) * 512 + 512].rearrange("p (e x) -> p e x", e=8)
                        v.tensor_tensor(K.misc[:, 512 + (iv2 % 4) * 256:512 + (iv2 % 4) * 256 + 256].rearrange("p (e x) -> p e x", e=4),
                                        sc3[:, 0:4, :], sc3[:, 4:8, :], ALU.add)
                    for (iv2, hl, t, h) in infos:
                        a1 = K.misc[:, 512 + (iv2 % 4) * 256:512 + (iv2 % 4) * 256 + 256].rearrange("p (e x) -> p e x", e=4)
                        v.tensor_tensor(K.expt[:, :].bitcast(F32)[:, (iv2 % 4) * 128:(iv2 % 4) * 128 + 128].rearrange("p (e x) -> p e x", e=2),
                                        a1[:, 0:2, :], a1[:, 2:4, :], ALU.add)
                    for (iv2, hl, t, h) in infos:
                        a2 = K.expt[:, :].bitcast(F32)[:, (iv2 % 4) * 128:(iv2 % 4) * 128 + 128]
                        v.tensor_tensor(K.vt[:, (t * 4 + hl) * 65:(t * 4 + hl) * 65 + 64],
                                        a2[:, 0:64], a2[:, 64:128], ALU.add).then_inc(vgd, 1)
                    iv += 4


def emit_l1ffn_oside(K, tag, wo):
    tag = K.tag(tag)
    nc = K.nc
    st = K.stats
    with ExitStack() as ps:
        e = ps.enter_context
        p_o = e(nc.psum_tensor(f"poA_{tag}", [128, 4 * 512], F32))
        p_t = e(nc.psum_tensor(f"potr_{tag}", [128, 2 * 512], BF16))
        wld = e(nc.semaphore(f"wld_{tag}"))
        otr = e(nc.semaphore(f"otr_{tag}"))
        oev = e(nc.semaphore(f"oev_{tag}"))
        ogd = e(nc.semaphore(f"ogd_{tag}"))
        ogt = e(nc.semaphore(f"ogt_{tag}"))
        oge = e(nc.semaphore(f"oge_{tag}"))
        mmo = e(nc.semaphore(f"mmo_{tag}"))
        xsd = e(nc.semaphore(f"xsd_{tag}"))
        yd = e(nc.semaphore(f"yd_{tag}"))
        with nc.Block(f"oside_{tag}") as b:
            @b.gpsimd
            def _(g):
                n = 0
                for tg in range(8):
                    for h in range(16):
                        if n >= 2:
                            g.wait_ge(mmo, n - 1)
                        g.dma_start(out=K.wbuf[:, (n % 2) * 4096:(n % 2) * 4096 + 4096]
                                    .rearrange("p (c m) -> p c m", c=4),
                                    in_=wo[h].rearrange("(c p) m -> p c m", p=128)).then_inc(wld, 16)
                        n += 1

            @b.tensor
            def _(tn):
                n = 0
                for tg in range(8):
                    if tg >= 1:
                        tn.wait_ge(yd, 4 * tg)
                    for h in range(16):
                        P, r0 = (h % 8) // 2, 64 * (h % 2)
                        gp = h // 2
                        if n >= 1:
                            tn.wait_ge(oge, n)
                        for j in range(2):
                            t = tg * 2 + j
                            ins = tn.matmul(p_t[:, j * 512: j * 512 + 64],
                                            K.fm2[r0:r0 + 64, gp * 2048 + t * 128: gp * 2048 + (t + 1) * 128],
                                            K.ident[r0:r0 + 64, r0:r0 + 64], is_transpose=True,
                                            start=True, stop=True)
                        ins.then_inc(otr, 1)
                        tn.wait_ge(ogd, n + 1)
                        for j in range(2):
                            for c in range(4):
                                ins = tn.matmul(p_t[:, j * 512 + c * 128: j * 512 + (c + 1) * 128],
                                                K.expt[:, 512 + j * 512 + c * 128: 512 + j * 512 + (c + 1) * 128],
                                                K.ident[:, :], is_transpose=True,
                                                start=(c == 0), stop=(c == 3))
                        ins.then_inc(ogt, 1)
                        tn.wait_ge(oge, n + 1)
                        tn.wait_ge(wld, 16 * (n + 1))
                        for j in range(2):
                            for dg in range(2):
                                for c in range(4):
                                    ins = tn.matmul(
                                        p_o[:, (j * 2 + dg) * 512:(j * 2 + dg) * 512 + 512],
                                        K.qk[:, j * 512 + c * 128: j * 512 + (c + 1) * 128],
                                        K.wbuf[:, (n % 2) * 4096 + c * 1024 + dg * 512: (n % 2) * 4096 + c * 1024 + (dg + 1) * 512],
                                        start=(h == 0 and c == 0), stop=(h == 15 and c == 3))
                        ins.then_inc(mmo, 1)
                        n += 1

            @b.scalar
            def _(s):
                n = 0
                for tg in range(8):
                    for h in range(16):
                        s.wait_ge(otr, n + 1)
                        if n >= 1:
                            s.wait_ge(ogd, n)
                        s.copy(K.expt[:, 0:64], p_t[:, 0:64])
                        s.copy(K.expt[:, 64:128], p_t[:, 512:576]).then_inc(oev, 1)
                        n += 1
                    if tg >= 1:
                        s.wait_ge(yd, 4 * tg)
                    for j in range(2):
                        t = tg * 2 + j
                        for dg in range(2):
                            s.activation(K.scr[:, (j * 2 + dg) * 512:(j * 2 + dg) * 512 + 512],
                                         K.resid[:, t * 1024 + dg * 512: t * 1024 + (dg + 1) * 512],
                                         AF.Identity,
                                         bias=st[:, ST_SV_NMR + t:ST_SV_NMR + t + 1],
                                         scale=st[:, ST_SV_R + t:ST_SV_R + t + 1]).then_inc(xsd, 1)

            @b.vector
            def _(v):
                n = 0
                for tg in range(8):
                    for h in range(16):
                        v.wait_ge(oev, n + 1)
                        if n >= 1:
                            v.wait_ge(ogt, n)
                        for j in range(2):
                            t = tg * 2 + j
                            gsl = K.xtok[:, t * 128 + h * 8: t * 128 + h * 8 + 8].unsqueeze(2).to_broadcast((128, 8, 64))
                            ob = K.expt[:, j * 64:(j + 1) * 64].unsqueeze(1).to_broadcast((128, 8, 64))
                            ins = v.tensor_tensor(
                                K.expt[:, 512 + j * 512:512 + (j + 1) * 512].rearrange("p (e x) -> p e x", e=8),
                                ob, gsl, ALU.mult)
                        ins.then_inc(ogd, 1)
                        v.wait_ge(ogt, n + 1)
                        v.tensor_copy(K.qk[:, 0:512], p_t[:, 0:512])
                        v.tensor_copy(K.qk[:, 512:1024], p_t[:, 512:1024]).then_inc(oge, 1)
                        n += 1
                    for j in range(2):
                        t = tg * 2 + j
                        for dg in range(2):
                            v.wait_ge(mmo, tg * 16 + 16)
                            v.wait_ge(xsd, tg * 4 + j * 2 + dg + 1)
                            v.scalar_tensor_tensor(
                                K.fm1[:, t * 1024 + dg * 512: t * 1024 + (dg + 1) * 512],
                                p_o[:, (j * 2 + dg) * 512:(j * 2 + dg) * 512 + 512],
                                1.0, K.scr[:, (j * 2 + dg) * 512:(j * 2 + dg) * 512 + 512],
                                ALU.mult, ALU.add).then_inc(yd, 1)


def emit_ln_tok(K, tag, gb_dram, mode):
    tag = K.tag(tag)
    nc = K.nc
    src = K.fm1
    K.ld_n += 16
    ld_tgt = K.ld_n
    vtf = K.vt[:, 0:4096].bitcast(F32)
    gbuf = vtf[:, 0:1024]
    bbuf = vtf[:, 1024:2048]
    with ExitStack() as ps:
        e = ps.enter_context
        p_tr = e(nc.psum_tensor(f"plt_{tag}", [128, 2 * 1024], BF16))
        stv = e(nc.semaphore(f"stv_{tag}"))
        sts = e(nc.semaphore(f"sts_{tag}"))
        st2 = e(nc.semaphore(f"st2_{tag}"))
        nrm = e(nc.semaphore(f"nrm_{tag}"))
        afd = e(nc.semaphore(f"afd_{tag}"))
        tp = e(nc.semaphore(f"tp_{tag}"))
        ev = e(nc.semaphore(f"ev_{tag}"))
        gad = e(nc.semaphore(f"gad_{tag}"))

        st = K.stats
        sums = st[:, ST_SUMS:ST_SUMS + 16]
        sqs = st[:, ST_SQS:ST_SQS + 16]
        negmean = st[:, ST_NEGM:ST_NEGM + 16]
        m2 = st[:, ST_M2:ST_M2 + 16]
        var = st[:, ST_VAR:ST_VAR + 16]
        rstd = st[:, ST_RSTD:ST_RSTD + 16]
        negmr = st[:, ST_NEGMR:ST_NEGMR + 16]
        eps = st[:, ST_EPS:ST_EPS + 1]

        with nc.Block(f"lnt_{tag}") as b:
            @b.sync
            def _(sy):
                sy.dma_start(out=vtf[:, 0:2048].rearrange("p (a x) -> p a x", a=2),
                             in_=gb_dram.rearrange("a p x -> p a x")).then_inc(K.ld, 16)

            @b.vector
            def _(v):
                for t in range(NT):
                    v.reduce_sum(sums[:, t:t + 1], src[:, t * 1024:(t + 1) * 1024], axis=AX.X)
                    v.scalar_tensor_tensor(K.scr[:, (t % 2) * 1024:(t % 2) * 1024 + 1024],
                                           src[:, t * 1024:(t + 1) * 1024], 1.0,
                                           src[:, t * 1024:(t + 1) * 1024],
                                           ALU.mult, ALU.mult,
                                           accum_out=sqs[:, t:t + 1])
                for i in range(4):
                    v.tensor_scalar(qcol(negmean, i), qcol(sums, i), -1.0 / 1024, None, ALU.mult)
                for i in range(4):
                    v.tensor_tensor(qcol(m2, i), qcol(negmean, i), qcol(negmean, i), ALU.mult)
                for i in range(4):
                    ins = v.scalar_tensor_tensor(qcol(var, i), qcol(sqs, i), 1.0 / 1024,
                                                 qcol(m2, i), ALU.mult, ALU.subtract)
                ins.then_inc(stv, 1)
                v.wait_ge(sts, 1)
                for i in range(4):
                    ins = v.tensor_tensor(qcol(negmr, i), qcol(negmean, i), qcol(rstd, i), ALU.mult)
                ins.then_inc(st2, 1)
                v.wait_ge(K.ld, ld_tgt)
                for hq in range(0, 32, 4):
                    quad = [hq, hq + 1, hq + 2, hq + 3]
                    v.wait_ge(nrm, (hq + 3) // 2 + 1)
                    if mode == "add_resid" and hq >= 4:
                        v.wait_ge(gad, hq - 3)
                    for i in quad:
                        t, dg = i // 2, i % 2
                        v.tensor_tensor(K.scr[:, (i % 4) * 512:(i % 4) * 512 + 512],
                                        K.xtok[:, (t % 2) * 1024 + dg * 512:(t % 2) * 1024 + (dg + 1) * 512],
                                        gbuf[:, dg * 512:(dg + 1) * 512], ALU.mult)
                    for i in quad:
                        t, dg = i // 2, i % 2
                        if mode == "store":
                            ins = v.tensor_tensor(src[:, t * 1024 + dg * 512: t * 1024 + (dg + 1) * 512],
                                                  K.scr[:, (i % 4) * 512:(i % 4) * 512 + 512],
                                                  bbuf[:, dg * 512:(dg + 1) * 512], ALU.add)
                        else:
                            ins = v.tensor_tensor(K.scr[:, (i % 4) * 512:(i % 4) * 512 + 512],
                                                  K.scr[:, (i % 4) * 512:(i % 4) * 512 + 512],
                                                  bbuf[:, dg * 512:(dg + 1) * 512], ALU.add)
                        ins.then_inc(afd, 1)
                if mode == "store":
                    for t in range(NT):
                        v.wait_ge(tp, t + 1)
                        pb = p_tr[:, (t % 2) * 1024:(t % 2) * 1024 + 1024]
                        dv = K.fm2[:, :].rearrange("p (c x) -> p c x", c=ND)[:, :, t * 128:(t + 1) * 128]
                        v.tensor_copy(dv, pb.rearrange("p (c x) -> p c x", c=ND)).then_inc(ev, 1)

            @b.scalar
            def _(s):
                s.wait_ge(stv, 1)
                for i in range(4):
                    s.activation(qcol(rstd, i), qcol(var, i), AF.Ln, bias=eps)
                for i in range(4):
                    ins = s.activation(qcol(rstd, i), qcol(rstd, i), AF.Exp, scale=-0.5)
                ins.then_inc(sts, 1)
                s.wait_ge(st2, 1)
                for t in range(NT):
                    if t >= 2:
                        s.wait_ge(afd, 2 * (t - 1))
                    s.activation(K.xtok[:, (t % 2) * 1024:(t % 2) * 1024 + 1024],
                                 src[:, t * 1024:(t + 1) * 1024], AF.Identity,
                                 bias=negmr[:, t:t + 1], scale=rstd[:, t:t + 1]).then_inc(nrm, 1)

            if mode == "store":
                @b.tensor
                def _(tn):
                    for t in range(NT):
                        tn.wait_ge(afd, 2 * (t + 1))
                        if t >= 2:
                            tn.wait_ge(ev, t - 1)
                        pb = p_tr[:, (t % 2) * 1024:(t % 2) * 1024 + 1024]
                        for c in range(ND):
                            ins = tn.matmul(pb[:, c * 128:(c + 1) * 128],
                                            src[:, t * 1024 + c * 128: t * 1024 + (c + 1) * 128],
                                            K.ident[:, :], is_transpose=True,
                                            start=(c == 0), stop=(c == ND - 1))
                        ins.then_inc(tp, 1)
            else:
                @b.gpsimd
                def _(g):
                    for i in range(32):
                        t, dg = i // 2, i % 2
                        g.wait_ge(afd, i + 1)
                        g.tensor_add(K.resid[:, t * 1024 + dg * 512: t * 1024 + (dg + 1) * 512],
                                     K.resid[:, t * 1024 + dg * 512: t * 1024 + (dg + 1) * 512],
                                     K.scr[:, (i % 4) * 512:(i % 4) * 512 + 512]).then_inc(gad, 1)


def emit_l1ffn_ffn(K, tag, w1, w2, b1_off, b2row):
    tag = K.tag(tag)
    nc = K.nc
    frag = ([K.qk[:, c * 512:(c + 1) * 512] for c in range(16)]
            + [K.vt[:, c * 512:(c + 1) * 512] for c in range(8)]
            + [K.expt[:, c * 512:(c + 1) * 512] for c in range(3)]
            + [K.xtok[:, c * 512:(c + 1) * 512] for c in range(4)]
            + [K.misc[:, 0:256].bitcast(BF16)])
    n_w = 1 + 4 * 12
    K.lw_n += n_w * 16
    lw_base = K.lw_n - n_w * 16
    with ExitStack() as ps:
        e = ps.enter_context
        p_h = e(nc.psum_tensor(f"ph_{tag}", [128, 4 * 512], F32))
        p_y = e(nc.psum_tensor(f"py_{tag}", [128, 4 * 512], F32))
        mmh = e(nc.semaphore(f"mmh_{tag}"))
        hev = e(nc.semaphore(f"hev_{tag}"))
        mmy = e(nc.semaphore(f"mmy_{tag}"))
        yev = e(nc.semaphore(f"yev_{tag}"))

        def w1_slab_idx(tc, ocg):
            return 1 + tc * 12 + ocg

        def w2_slab_idx(tc, dg):
            return 1 + tc * 12 + 8 + dg

        with nc.Block(f"ffn_{tag}") as b:
            @b.gpsimd
            def _(g):
                g.dma_start(out=K.brow[0:1, 0:1024], in_=b2row).then_inc(K.lw, 16)
                for tc in range(4):
                    for ocg in range(8):
                        si = tc * 8 + ocg
                        if si >= 2:
                            g.wait_ge(mmh, (si - 1) * 4)
                        if tc >= 1 and ocg < 2:
                            g.wait_ge(mmy, tc * 16)
                        g.dma_start(out=K.wbuf[:, (si % 2) * 4096:(si % 2) * 4096 + 4096]
                                    .rearrange("p (c m) -> p c m", c=ND),
                                    in_=w1[:, ocg * 512:(ocg + 1) * 512]
                                    .rearrange("(c p) m -> p c m", p=128)).then_inc(K.lw, 16)
                    for dg in range(4):
                        di = tc * 4 + dg
                        g.wait_ge(mmh, 32 * (tc + 1))
                        if di >= 1:
                            g.wait_ge(mmy, di * 4)
                        g.dma_start(out=K.wbuf[:, 0:8192].rearrange("p (c m) -> p c m", c=32),
                                    in_=w2[:, dg * 256:(dg + 1) * 256]
                                    .rearrange("(c p) m -> p c m", p=128)).then_inc(K.lw, 16)

            @b.tensor
            def _(tn):
                nm = 0
                ny = 0
                for tc in range(4):
                    for ocg in range(8):
                        tn.wait_ge(K.lw, lw_base + 16 * (w1_slab_idx(tc, ocg) + 1))
                        for f4 in range(4):
                            if nm >= 4:
                                tn.wait_ge(hev, nm - 3)
                            if tc >= 1:
                                tn.wait_ge(mmy, (tc - 1) * 16 + 16)
                            pb = p_h[:, (nm % 4) * 512:(nm % 4) * 512 + 512]
                            woff = ((tc * 8 + ocg) % 2) * 4096
                            for kc in range(ND):
                                ins = tn.matmul(pb,
                                                K.wbuf[:, woff + kc * 512 + f4 * 128: woff + kc * 512 + (f4 + 1) * 128],
                                                K.fm2[:, kc * 2048 + tc * 512: kc * 2048 + (tc + 1) * 512],
                                                start=(kc == 0), stop=(kc == ND - 1))
                            ins.then_inc(mmh, 1)
                            nm += 1
                    for dg in range(4):
                        tn.wait_ge(K.lw, lw_base + 16 * (w2_slab_idx(tc, dg) + 1))
                        tn.wait_ge(hev, tc * 32 + 32)
                        for tt in range(4):
                            if ny >= 4:
                                tn.wait_ge(yev, ny - 3)
                            pbу = p_y[:, (ny % 4) * 512:(ny % 4) * 512 + 256]
                            for ffc in range(32):
                                tn.matmul(pbу,
                                          frag[ffc][:, tt * 128:(tt + 1) * 128],
                                          K.wbuf[:, ffc * 256:(ffc + 1) * 256],
                                          start=(ffc == 0), stop=False)
                            tn.matmul(pbу, K.ones_rb[0:1, 0:128],
                                      K.brow[0:1, dg * 256:(dg + 1) * 256],
                                      start=False, stop=True).then_inc(mmy, 1)
                            ny += 1

            @b.scalar
            def _(s):
                nm = 0
                for tc in range(4):
                    for ocg in range(8):
                        for f4 in range(4):
                            ffc = ocg * 4 + f4
                            s.wait_ge(mmh, nm + 1)
                            if tc >= 1:
                                s.wait_ge(mmy, (tc - 1) * 16 + 16)
                            s.activation(frag[ffc], p_h[:, (nm % 4) * 512:(nm % 4) * 512 + 512],
                                         AF.Relu, bias=K.pvec[:, b1_off + ffc:b1_off + ffc + 1]
                                         ).then_inc(hev, 1)
                            nm += 1

            @b.vector
            def _(v):
                ny = 0
                for tc in range(4):
                    for dg in range(4):
                        for tt in range(4):
                            v.wait_ge(mmy, ny + 1)
                            t = tc * 4 + tt
                            sl = K.fm1[:, t * 1024 + dg * 256: t * 1024 + (dg + 1) * 256]
                            v.scalar_tensor_tensor(sl, p_y[:, (ny % 4) * 512:(ny % 4) * 512 + 256],
                                                   1.0, sl, ALU.mult, ALU.add).then_inc(yev, 1)
                            ny += 1


def emit_l1_ffn(K, w):
    emit_ln(K, "1ff", save=True)
    K.barrier()
    if DBG_L1F == "ln":
        return
    emit_gate(K, "1ffv", w["wsel_v"])
    K.barrier()
    if DBG_L1F == "gate":
        return
    for qtr in range(4):
        emit_l1ffn_qkv_quarter(K, "1ff", qtr, w["wq"][qtr], w["wk"][qtr], w["wv4"][qtr],
                               w["ctab"], w["stab"])
        K.barrier()
        if DBG_L1F == f"qkv{qtr}":
            return
        emit_attn_core_quarter(K, "1ff", qtr, scale=0.125)
        K.barrier()
        if DBG_L1F == f"attn{qtr}":
            return
    emit_gate(K, "1ffo", w["wsel_o"])
    K.barrier()
    if DBG_L1F == "gateo":
        return
    emit_l1ffn_oside(K, "1ffo", w["wo16"])
    K.barrier()
    if DBG_L1F == "oside":
        return
    emit_ln_tok(K, "1ffl1", w["gb1"], "store")
    K.barrier()
    if DBG_L1F == "ln1":
        return
    emit_l1ffn_ffn(K, "1ffn", w["w1"], w["w2"], PV["l1ffn_b1"], w["b2row"])
    K.barrier()
    if DBG_L1F == "ffn":
        return
    emit_ln_tok(K, "1ffl2", w["gb2"], "add_resid")
    K.barrier()


def emit_proj_add(K, tag, wproj, bproj_row, add_z=False, src=None, n_out=D):
    tag = K.tag(tag)
    nc = K.nc
    src = src if src is not None else K.fm2
    K.lw_n += (2 if bproj_row is not None else 1) * 16
    lw_tgt = K.lw_n
    with ExitStack() as ps:
        e = ps.enter_context
        p_o = e(nc.psum_tensor(f"po_{tag}", [128, 4 * 512], F32))
        mmo = e(nc.semaphore(f"mmo_{tag}"))
        zs = e(nc.semaphore(f"zs_{tag}"))
        sm = e(nc.semaphore(f"sm_{tag}"))
        gad = e(nc.semaphore(f"gad_{tag}"))
        st = K.stats
        with nc.Block(f"proj_{tag}") as b:
            @b.gpsimd
            def _(g):
                g.dma_start(out=K.wbuf[:, 0:8192].rearrange("p (c m) -> p c m", c=ND),
                            in_=wproj.rearrange("(c p) m -> p c m", p=128)).then_inc(K.lw, 16)
                if bproj_row is not None:
                    g.dma_start(out=K.brow[0:1, 0:1024], in_=bproj_row).then_inc(K.lw, 16)
                for i in range(NT * 2):
                    g.wait_ge(sm, i + 1)
                    t, dg = i // 2, i % 2
                    g.tensor_add(K.resid[:, t * 1024 + dg * 512: t * 1024 + (dg + 1) * 512],
                                 K.resid[:, t * 1024 + dg * 512: t * 1024 + (dg + 1) * 512],
                                 K.scr[:, (2 + i % 2) * 512:(2 + i % 2) * 512 + 512]).then_inc(gad, 1)

            @b.tensor
            def _(tn):
                tn.wait_ge(K.lw, lw_tgt)
                for i in range(NT * 2):
                    t, dg = i // 2, i % 2
                    if i >= 4:
                        tn.wait_ge(sm, i - 3)
                    pb = p_o[:, (i % 4) * 512:(i % 4) * 512 + 512]
                    for c in range(ND):
                        ins = tn.matmul(pb,
                                        src[:, c * 2048 + t * 128: c * 2048 + (t + 1) * 128],
                                        K.wbuf[:, c * 1024 + dg * 512: c * 1024 + (dg + 1) * 512],
                                        start=(c == 0),
                                        stop=(c == ND - 1 and bproj_row is None))
                    if bproj_row is not None:
                        ins = tn.matmul(pb, K.ones_rb[0:1, 0:128],
                                        K.brow[0:1, dg * 512:(dg + 1) * 512],
                                        start=False, stop=True)
                    ins.then_inc(mmo, 1)

            @b.scalar
            def _(s):
                if add_z:
                    for i in range(NT * 2):
                        t, dg = i // 2, i % 2
                        if i >= 2:
                            s.wait_ge(sm, i - 1)
                        s.activation(K.scr[:, (i % 2) * 512:(i % 2) * 512 + 512],
                                     K.resid[:, t * 1024 + dg * 512: t * 1024 + (dg + 1) * 512],
                                     AF.Identity,
                                     bias=st[:, ST_SV_NMR + t:ST_SV_NMR + t + 1],
                                     scale=st[:, ST_SV_R + t:ST_SV_R + t + 1]).then_inc(zs, 1)

            @b.vector
            def _(v):
                for i in range(NT * 2):
                    v.wait_ge(mmo, i + 1)
                    pb = p_o[:, (i % 4) * 512:(i % 4) * 512 + 512]
                    if add_z:
                        v.wait_ge(zs, i + 1)
                        if i >= 2:
                            v.wait_ge(gad, i - 1)
                        v.scalar_tensor_tensor(K.scr[:, (2 + i % 2) * 512:(2 + i % 2) * 512 + 512],
                                               pb, 1.0, K.scr[:, (i % 2) * 512:(i % 2) * 512 + 512],
                                               ALU.mult, ALU.add).then_inc(sm, 1)
                    else:
                        if i >= 2:
                            v.wait_ge(gad, i - 1)
                        v.tensor_copy(K.scr[:, (2 + i % 2) * 512:(2 + i % 2) * 512 + 512],
                                      pb).then_inc(sm, 1)


def emit_l0_attn_b(K, w):
    emit_ln(K, "ab_o", dst=K.fm1, save=True, no_out=True)
    K.barrier()
    emit_ln(K, "ab_i", dst=K.fm1, gb=(w["gb_g"], w["gb_b"]), inner_of_saved=True)
    K.barrier()
    for qtr in range(4):
        emit_qkv_quarter(K, "ab", qtr, w["wq"][qtr], w["wk"][qtr], w["wv"][qtr],
                         bq_off=w["bq"] + 2 * qtr, bk_off=w["bk"] + 2 * qtr,
                         bv_row=w["bv_rows"][qtr])
        K.barrier()
        emit_attn_core_quarter(K, "ab", qtr, scale=0.125)
        K.barrier()
    emit_proj_add(K, "ab", w["wproj"], w["bproj_row"], add_z=True)
    K.barrier()



def build_program(seq, debug=False):
    nc = bass.Bass("TRN2", target_bir_lowering=False)

    def din(name, shape, dt=F32):
        return nc.dram_tensor(name, list(shape), dt, kind="ExternalInput").ap()

    x = din("x", [S, D])
    ident = din("c_ident", [128, 128], BF16)
    identf = din("c_identf", [128, 128])
    pvec_in = din("c_pvec", [128, PVEC_COLS])
    out = nc.dram_tensor("out", [S, D], F32, kind="ExternalOutput").ap()

    dr = {}
    if "l0aa" in seq:
        dr["l0aa_w"] = din("l0aa_w", [D, D])
        dr["l0aa_b"] = din("l0aa_b", [1, D])
    if "l0ab" in seq:
        dr["l0ab_wq"] = din("l0ab_wq", [4, D, 256])
        dr["l0ab_wk"] = din("l0ab_wk", [4, D, 256])
        dr["l0ab_wv"] = din("l0ab_wv", [4, D, 256])
        dr["l0ab_bv"] = din("l0ab_bv", [4, 1, 256])
        dr["l0ab_wproj"] = din("l0ab_wproj", [D, D])
        dr["l0ab_bproj"] = din("l0ab_bproj", [1, D])
    if "l1aa" in seq:
        dr["l1aa_wq"] = din("l1aa_wq", [4, D, 256])
        dr["l1aa_wk"] = din("l1aa_wk", [4, D, 256])
        dr["l1aa_wv"] = din("l1aa_wv", [4, D, 256])
        dr["l1aa_wo"] = din("l1aa_wo", [D, D])
    if "l0ffn" in seq:
        dr["l0f_wq"] = din("l0f_wq", [4, D, 256])
        dr["l0f_wk"] = din("l0f_wk", [4, D, 256])
        dr["l0f_wv"] = din("l0f_wv", [4, D, 256])
        dr["l0f_wo"] = din("l0f_wo", [D, D])
        dr["l0f_wr"] = din("l0f_wr", [D, D])
        dr["l0f_perev"] = din("l0f_perev", [D, S])
        dr["l0f_wg"] = din("l0f_wg", [D, 8])
        dr["l0f_we1"] = din("l0f_we1", [8, D, 4096])
        dr["l0f_we2"] = din("l0f_we2", [8, 4096, D])
        dr["l0f_be2"] = din("l0f_be2", [8, 1, D])
        dr["l0f_gb1"] = din("l0f_gb1", [2, 128, D])
        dr["l0f_gb2"] = din("l0f_gb2", [2, 128, D])
        dr["l0f_gb3"] = din("l0f_gb3", [2, 128, D])
        dr["l0f_wff1"] = din("l0f_wff1", [D, 4096])
        dr["l0f_wff2"] = din("l0f_wff2", [4096, D])
        dr["l0f_bff2"] = din("l0f_bff2", [1, D])
        dr["l0f_wout"] = din("l0f_wout", [D, D])
        K_drrr = nc.dram_tensor("i_rr", [8, 128, 2048], BF16, kind="Internal")
        K_drz = nc.dram_tensor("i_z", [2 * 512 * 2560], BF16, kind="Internal")
        K_we1c = nc.dram_tensor("i_we1c", [8, D, 4096], BF16, kind="Internal")
        K_we2c = nc.dram_tensor("i_we2c", [8, 4096, D], BF16, kind="Internal")
    if "l1ffn" in seq:
        dr["l1f_wq"] = din("l1f_wq", [4, D, 256])
        dr["l1f_wk"] = din("l1f_wk", [4, D, 256])
        dr["l1f_wv"] = din("l1f_wv", [4, 4, D, 512])
        dr["l1f_wo"] = din("l1f_wo", [16, 512, D])
        dr["l1f_selv"] = din("l1f_selv", [D, 128])
        dr["l1f_selo"] = din("l1f_selo", [D, 128])
        dr["l1f_ctab"] = din("l1f_ctab", [128, 1024])
        dr["l1f_stab"] = din("l1f_stab", [128, 1024])
        dr["l1f_gb1"] = din("l1f_gb1", [2, 128, D])
        dr["l1f_gb2"] = din("l1f_gb2", [2, 128, D])
        dr["l1f_w1"] = din("l1f_w1", [D, 4096])
        dr["l1f_w2"] = din("l1f_w2", [4096, D])
        dr["l1f_b2"] = din("l1f_b2", [1, D])
    if "l1ab" in seq:
        dr["l1ab_wq"] = din("l1ab_wq", [4, D, 256])
        dr["l1ab_wk"] = din("l1ab_wk", [4, D, 256])
        dr["l1ab_wv"] = din("l1ab_wv", [4, D, 256])
        dr["l1ab_wproj"] = din("l1ab_wproj", [D, D])
        dr["l1ab_decay"] = din("l1ab_decay", [1, 16])
        dr["c_l1p"] = din("c_l1p", [20, 128, 512])

    with ExitStack() as stack:
        K = Kern(nc, stack)
        K.debug = debug
        if "l0ffn" in seq:
            K.dr_rr = K_drrr.ap()
            K.dr_z = K_drz
            K.we1c = K_we1c.ap()
            K.we2c = K_we2c.ap()
        consts = {"x": x, "c_ident": ident, "c_identf": identf, "c_pvec": pvec_in}
        emit_init(K, consts)
        K.barrier()
        n_l0ffn = [0]
        for sub in seq:
            if sub == "l0aa":
                emit_ln(K, "l0aa")
                K.barrier()
                emit_l0_attn_a(K, dr["l0aa_w"], dr["l0aa_b"])
                K.barrier()
            elif sub == "l0ab":
                w = {
                    "wq": [dr["l0ab_wq"][h] for h in range(4)],
                    "wk": [dr["l0ab_wk"][h] for h in range(4)],
                    "wv": [dr["l0ab_wv"][h] for h in range(4)],
                    "bv_rows": [dr["l0ab_bv"][h] for h in range(4)],
                    "bq": PV["l0ab_bq"],
                    "bk": PV["l0ab_bk"],
                    "gb_g": PV["l0ab_g"], "gb_b": PV["l0ab_b"],
                    "wproj": dr["l0ab_wproj"], "bproj_row": dr["l0ab_bproj"],
                }
                emit_l0_attn_b(K, w)
            elif sub == "l1aa":
                w = {
                    "wq": [dr["l1aa_wq"][h] for h in range(4)],
                    "wk": [dr["l1aa_wk"][h] for h in range(4)],
                    "wv": [dr["l1aa_wv"][h] for h in range(4)],
                    "wo": dr["l1aa_wo"],
                }
                emit_l1_attn_a(K, w)
            elif sub == "l0ffn" or sub == "l0ffn2":
                w = {
                    "wq": [dr["l0f_wq"][h] for h in range(4)],
                    "wk": [dr["l0f_wk"][h] for h in range(4)],
                    "wv": [dr["l0f_wv"][h] for h in range(4)],
                    "wo": dr["l0f_wo"], "wr": dr["l0f_wr"], "perev": dr["l0f_perev"],
                    "wg": dr["l0f_wg"],
                    "we1": [dr["l0f_we1"][h] for h in range(8)],
                    "we2": [dr["l0f_we2"][h] for h in range(8)],
                    "we1c": [K.we1c[h] for h in range(8)],
                    "we2c": [K.we2c[h] for h in range(8)],
                    "be2": [dr["l0f_be2"][h] for h in range(8)],
                    "gb1": dr["l0f_gb1"], "gb2": dr["l0f_gb2"], "gb3": dr["l0f_gb3"],
                    "wff1": dr["l0f_wff1"], "wff2": dr["l0f_wff2"],
                    "bff2row": dr["l0f_bff2"], "wout": dr["l0f_wout"],
                }
                emit_l0_ffn(K, w, n_l0ffn[0])
                n_l0ffn[0] += 1
            elif sub == "l1ffn" or sub == "l1ffn2":
                w = {
                    "wq": [dr["l1f_wq"][h] for h in range(4)],
                    "wk": [dr["l1f_wk"][h] for h in range(4)],
                    "wv4": [[dr["l1f_wv"][q][h] for h in range(4)] for q in range(4)],
                    "wo16": [dr["l1f_wo"][h] for h in range(16)],
                    "wsel_v": dr["l1f_selv"], "wsel_o": dr["l1f_selo"],
                    "ctab": dr["l1f_ctab"], "stab": dr["l1f_stab"],
                    "gb1": dr["l1f_gb1"], "gb2": dr["l1f_gb2"],
                    "w1": dr["l1f_w1"], "w2": dr["l1f_w2"], "b2row": dr["l1f_b2"],
                }
                emit_l1_ffn(K, w)
            elif sub == "l1ab":
                w = {
                    "wq": [dr["l1ab_wq"][h] for h in range(4)],
                    "wk": [dr["l1ab_wk"][h] for h in range(4)],
                    "wv": [dr["l1ab_wv"][h] for h in range(4)],
                    "wproj": dr["l1ab_wproj"],
                    "decay": dr["l1ab_decay"],
                    "l1p": dr["c_l1p"],
                }
                emit_l1_attn_b(K, w)
            else:
                raise NotImplementedError(sub)
        emit_store(K, out)
    return nc


PV = {}
_pv_next = [0]
def _pv(name, n):
    PV[name] = _pv_next[0]
    _pv_next[0] += n
    return PV[name]

_pv("l0ab_g", 8); _pv("l0ab_b", 8)
_pv("l0ab_bq", 8); _pv("l0ab_bk", 8)
_pv("l1ffn_b1", 32)
_pv("l0f_u", 8); _pv("l0f_v", 8); _pv("l0f_dv", 8)
_pv("l0f_bff1", 32); _pv("l0f_be1", 256); _pv("l0f_dg", 128)
PVEC_COLS = 640


def make_l1p():
    o = (np.arange(20) - 3)[:, None, None] * 128
    dist = o + np.arange(512)[None, None, :] - np.arange(128)[None, :, None]
    out = np.where(dist >= 0, np.log1p(np.maximum(dist, 0)), 1e12)
    return out.astype(np.float32)


def fmajor_cols(vec):
    v = np.asarray(vec, np.float32).reshape(-1, 128)
    return v.T


def make_pvec(inputs):
    pv = np.zeros((128, PVEC_COLS), np.float32)
    def put(name, vec):
        c = fmajor_cols(vec)
        pv[:, PV[name]:PV[name] + c.shape[1]] = c
    if "L0_attn_b_ln_w" in inputs:
        put("l0ab_g", inputs["L0_attn_b_ln_w"])
        put("l0ab_b", inputs["L0_attn_b_ln_b"])
        bqkv = np.asarray(inputs["L0_attn_b_b_qkv"], np.float32)
        put("l0ab_bq", bqkv[0:D])
        put("l0ab_bk", bqkv[D:2 * D])
    if "L1_ffn_b1" in inputs:
        put("l1ffn_b1", inputs["L1_ffn_b1"])
    if "L0_ffn_u_bias" in inputs:
        ub = np.asarray(inputs["L0_ffn_u_bias"], np.float32)
        vb = np.asarray(inputs["L0_ffn_v_bias"], np.float32)
        pv[:, PV["l0f_u"]:PV["l0f_u"] + 8] = ub.reshape(8, 128).T
        pv[:, PV["l0f_v"]:PV["l0f_v"] + 8] = vb.reshape(8, 128).T
        put("l0f_bff1", inputs["L0_ffn_bff1"])
        be1 = np.asarray(inputs["L0_ffn_be1"], np.float32)
        pv[:, PV["l0f_be1"]:PV["l0f_be1"] + 256] = np.concatenate(
            [be1[e].reshape(32, 128).T for e in range(8)], axis=1)
    return pv


def make_inmaps(inputs, seq):
    x = np.asarray(inputs["L0_attn_a_x"], np.float32)
    base = {
        "c_ident": np.eye(128, dtype=BF),
        "c_identf": np.eye(128, dtype=np.float32),
        "c_pvec": make_pvec(inputs),
    }
    if "l0aa" in seq:
        base["l0aa_w"] = np.ascontiguousarray(np.asarray(inputs["L0_attn_a_W"], np.float32).T)
        base["l0aa_b"] = np.asarray(inputs["L0_attn_a_bias"], np.float32).reshape(1, D)
    if "l0ab" in seq:
        wqkv = np.asarray(inputs["L0_attn_b_W_qkv"], np.float32)
        bqkv = np.asarray(inputs["L0_attn_b_b_qkv"], np.float32)
        base["l0ab_wq"] = np.ascontiguousarray(
            np.stack([wqkv[:, 0 * D + h * 256: 0 * D + (h + 1) * 256] for h in range(4)]))
        base["l0ab_wk"] = np.ascontiguousarray(
            np.stack([wqkv[:, 1 * D + h * 256: 1 * D + (h + 1) * 256] for h in range(4)]))
        base["l0ab_wv"] = np.ascontiguousarray(
            np.stack([wqkv[:, 2 * D + h * 256: 2 * D + (h + 1) * 256] for h in range(4)]))
        base["l0ab_bv"] = np.ascontiguousarray(
            bqkv[2 * D:3 * D].reshape(4, 1, 256))
        base["l0ab_wproj"] = np.ascontiguousarray(np.asarray(inputs["L0_attn_b_W_proj"], np.float32))
        base["l0ab_bproj"] = np.asarray(inputs["L0_attn_b_b_proj"], np.float32).reshape(1, D)
    if "l1aa" in seq:
        for nm, key in (("wq", "L1_attn_a_wq"), ("wk", "L1_attn_a_wk"), ("wv", "L1_attn_a_wv")):
            wm = np.asarray(inputs[key], np.float32)
            base[f"l1aa_{nm}"] = np.ascontiguousarray(
                np.stack([wm[:, h * 256:(h + 1) * 256] for h in range(4)]))
        base["l1aa_wo"] = np.ascontiguousarray(np.asarray(inputs["L1_attn_a_wo"], np.float32))
    if "l0ffn" in seq:
        wqkv = np.asarray(inputs["L0_ffn_Wqkv"], np.float32)
        base["l0f_wq"] = np.ascontiguousarray(
            np.stack([wqkv[:, 0 * D + h * 256: 0 * D + (h + 1) * 256] for h in range(4)]))
        base["l0f_wk"] = np.ascontiguousarray(
            np.stack([wqkv[:, 1 * D + h * 256: 1 * D + (h + 1) * 256] for h in range(4)]))
        base["l0f_wv"] = np.ascontiguousarray(
            np.stack([wqkv[:, 2 * D + h * 256: 2 * D + (h + 1) * 256] for h in range(4)]))
        base["l0f_wo"] = np.ascontiguousarray(np.asarray(inputs["L0_ffn_Wo"], np.float32))
        base["l0f_wr"] = np.ascontiguousarray(np.asarray(inputs["L0_ffn_Wr"], np.float32))
        pos = np.arange(S - 1, -1, -1, dtype=np.float64)
        inv_freq = 1.0 / 10000.0 ** (np.arange(0, D, 2, dtype=np.float64) / D)
        sinus = pos[:, None] * inv_freq[None, :]
        pe = np.concatenate([np.sin(sinus), np.cos(sinus)], axis=-1)
        base["l0f_perev"] = np.ascontiguousarray(pe.T.astype(np.float32))
        base["l0f_wg"] = np.ascontiguousarray(np.asarray(inputs["L0_ffn_Wg"], np.float32))
        base["l0f_we1"] = np.ascontiguousarray(np.asarray(inputs["L0_ffn_We1"], np.float32))
        base["l0f_we2"] = np.ascontiguousarray(np.asarray(inputs["L0_ffn_We2"], np.float32))
        base["l0f_be2"] = np.ascontiguousarray(
            np.asarray(inputs["L0_ffn_be2"], np.float32).reshape(8, 1, D))
        for i, nm in ((1, "ln1"), (2, "ln2"), (3, "ln3")):
            base[f"l0f_gb{i}"] = np.ascontiguousarray(np.stack([
                np.broadcast_to(np.asarray(inputs[f"L0_ffn_{nm}_g"], np.float32), (128, D)),
                np.broadcast_to(np.asarray(inputs[f"L0_ffn_{nm}_b"], np.float32), (128, D))]))
        base["l0f_wff1"] = np.ascontiguousarray(np.asarray(inputs["L0_ffn_Wff1"], np.float32))
        base["l0f_wff2"] = np.ascontiguousarray(np.asarray(inputs["L0_ffn_Wff2"], np.float32))
        base["l0f_bff2"] = np.asarray(inputs["L0_ffn_bff2"], np.float32).reshape(1, D)
        base["l0f_wout"] = np.ascontiguousarray(np.asarray(inputs["L0_ffn_Wout"], np.float32))
    if "l1ffn" in seq:
        Wq = np.asarray(inputs["L1_ffn_Wq"], np.float32)
        Wk = np.asarray(inputs["L1_ffn_Wk"], np.float32)
        Wv = np.asarray(inputs["L1_ffn_Wv"], np.float32)
        Wo = np.asarray(inputs["L1_ffn_Wo"], np.float32)
        base["l1f_wq"] = np.ascontiguousarray(
            np.stack([np.concatenate([Wq[q * 4 + i] for i in range(4)], axis=1) for q in range(4)]))
        base["l1f_wk"] = np.ascontiguousarray(
            np.stack([np.concatenate([Wk[q * 4 + i] for i in range(4)], axis=1) for q in range(4)]))
        base["l1f_wv"] = np.ascontiguousarray(
            np.stack([np.stack([Wv[q * 4 + i].transpose(1, 0, 2).reshape(D, 512) for i in range(4)])
                      for q in range(4)]))
        base["l1f_wo"] = np.ascontiguousarray(Wo.reshape(16, 512, D))
        base["l1f_selv"] = np.ascontiguousarray(
            np.asarray(inputs["L1_ffn_Wsel_v"], np.float32).transpose(2, 0, 1).reshape(D, 128))
        base["l1f_selo"] = np.ascontiguousarray(
            np.asarray(inputs["L1_ffn_Wsel_o"], np.float32).transpose(2, 0, 1).reshape(D, 128))
        inv = 10000.0 ** (-np.arange(16, dtype=np.float64) / 16.0)
        ang = np.arange(2048, dtype=np.float64)[:, None] * inv[None, :]
        ct = np.zeros((128, 1024), np.float32)
        st_ = np.zeros((128, 1024), np.float32)
        for t in range(16):
            blk_c = np.cos(ang[t * 128:(t + 1) * 128]).astype(np.float32)
            blk_s = np.sin(ang[t * 128:(t + 1) * 128]).astype(np.float32)
            for j in range(4):
                ct[:, t * 64 + j * 16: t * 64 + (j + 1) * 16] = blk_c
                st_[:, t * 64 + j * 16: t * 64 + (j + 1) * 16] = blk_s
        base["l1f_ctab"] = ct
        base["l1f_stab"] = st_
        base["l1f_gb1"] = np.ascontiguousarray(np.stack([
            np.broadcast_to(np.asarray(inputs["L1_ffn_ln1_g"], np.float32), (128, D)),
            np.broadcast_to(np.asarray(inputs["L1_ffn_ln1_b"], np.float32), (128, D))]))
        base["l1f_gb2"] = np.ascontiguousarray(np.stack([
            np.broadcast_to(np.asarray(inputs["L1_ffn_ln2_g"], np.float32), (128, D)),
            np.broadcast_to(np.asarray(inputs["L1_ffn_ln2_b"], np.float32), (128, D))]))
        base["l1f_w1"] = np.ascontiguousarray(np.asarray(inputs["L1_ffn_W1"], np.float32))
        base["l1f_w2"] = np.ascontiguousarray(np.asarray(inputs["L1_ffn_W2"], np.float32))
        base["l1f_b2"] = np.asarray(inputs["L1_ffn_b2"], np.float32).reshape(1, D)
    if "l1ab" in seq:
        wat = np.asarray(inputs["L1_attn_b_w_attn"], np.float32)
        base["l1ab_wq"] = np.ascontiguousarray(
            np.stack([wat[:, 0 * D + h * 256: 0 * D + (h + 1) * 256] for h in range(4)]))
        base["l1ab_wk"] = np.ascontiguousarray(
            np.stack([wat[:, 1 * D + h * 256: 1 * D + (h + 1) * 256] for h in range(4)]))
        base["l1ab_wv"] = np.ascontiguousarray(
            np.stack([wat[:, 2 * D + h * 256: 2 * D + (h + 1) * 256] for h in range(4)]))
        base["l1ab_wproj"] = np.ascontiguousarray(np.asarray(inputs["L1_attn_b_w_proj"], np.float32))
        base["l1ab_decay"] = np.asarray(inputs["L1_attn_b_decay_raw"], np.float32).reshape(1, 16)
        base["c_l1p"] = make_l1p()
    maps = []
    for i in range(8):
        m = dict(base)
        m["x"] = np.ascontiguousarray(x[i])
        maps.append(m)
    return maps


def kernel(**inputs):
    seq = SUBS_ALL
    nc = build_program(seq)
    maps = make_inmaps(inputs, seq)
    res = run_bass_kernel_spmd(nc, maps, core_ids=list(range(8)))
    return np.stack([res.results[i]["out"] for i in range(8)], axis=0)
